# version 17
# baseline (speedup 1.0000x reference)
import numpy as np
from contextlib import ExitStack

import concourse.bass as bass
import concourse.mybir as mybir
from concourse.bass_utils import run_bass_kernel_spmd

F32 = mybir.dt.float32
BF16 = mybir.dt.bfloat16
AF = mybir.ActivationFunctionType
ALU = mybir.AluOpType

try:
    import ml_dtypes
    BF16_NP = ml_dtypes.bfloat16
except Exception:
    BF16_NP = None


class Cfg:
    def __init__(self, S=8192, seg_tiles=32, phases=3):
        self.PHASES = phases
        self.D = 1024
        self.DC = 8
        self.S = S
        self.NOWN = S // 2
        self.QT = 512
        self.NCH = self.NOWN // self.QT
        self.NST = [(2 * self.QT // 128) * (i + 1) for i in range(self.NCH)]
        self.CUM = [0]
        for n in self.NST:
            self.CUM.append(self.CUM[-1] + n)
        self.SEG = seg_tiles
        self.segs = []
        for i in range(self.NCH):
            lst, t = [], 0
            while t < self.NST[i]:
                ln = min(self.SEG, self.NST[i] - t)
                lst.append((t, ln))
                t += ln
            self.segs.append(lst)
        self.SH_CUM = [0]
        for i in range(self.NCH):
            self.SH_CUM.append(self.SH_CUM[-1] + 2 * len(self.segs[i]))
        self.VU = {}
        vu = 0
        for i in range(self.NCH):
            for si, (s0, ln) in enumerate(self.segs[i]):
                for h in range(2):
                    self.VU[(i, si, h)] = vu
                    vu += ln // 4
        self.VU_TOT = vu
        self.NKT = [n // 4 for n in self.NST]
        self.KT_CUM = [0]
        for n in self.NKT:
            self.KT_CUM.append(self.KT_CUM[-1] + n)
        self.KT_LASTSC = []
        for i in range(self.NCH):
            for kt in range(self.NKT[i]):
                self.KT_LASTSC.append(self.CUM[i] + 4 * kt + 4)
        self.NSUP = S // 512
        self.NSUPQ = self.NOWN // 512
        self.FF = 4096
        self.FFC = 32
        self.RK = 256


def build_nc(cfg: Cfg):
    nc = bass.Bass("TRN2", target_bir_lowering=False, dynamic_dma_scratch_size=8192,
                   detect_race_conditions=False)
    D, DC, S = cfg.D, cfg.DC, cfg.S
    NCH, QT, SEG = cfg.NCH, cfg.QT, cfg.SEG
    NSUP, NSUPQ, FFC = cfg.NSUP, cfg.NSUPQ, cfg.FFC

    xown = nc.dram_tensor("xown", [128, NCH, DC, 512], F32, kind="ExternalInput").ap()
    Wq = nc.dram_tensor("Wq", [128, DC, D], F32, kind="ExternalInput").ap()
    Wk = nc.dram_tensor("Wk", [128, DC, D], F32, kind="ExternalInput").ap()
    Wv = nc.dram_tensor("Wv", [128, DC, D], F32, kind="ExternalInput").ap()
    Wo = nc.dram_tensor("Wo", [128, DC, D], F32, kind="ExternalInput").ap()
    U1 = nc.dram_tensor("U1", [128, DC, cfg.RK], F32, kind="ExternalInput").ap()
    V1 = nc.dram_tensor("V1", [128, 2, cfg.FF], F32, kind="ExternalInput").ap()
    b1 = nc.dram_tensor("b1", [128, cfg.FFC], F32, kind="ExternalInput").ap()
    U2 = nc.dram_tensor("U2", [128, FFC, cfg.RK], F32, kind="ExternalInput").ap()
    V2 = nc.dram_tensor("V2", [128, 2, D], F32, kind="ExternalInput").ap()
    b2 = nc.dram_tensor("b2", [128, DC], F32, kind="ExternalInput").ap()
    masks_in = nc.dram_tensor("masks", [128, 8, QT], BF16, kind="ExternalInput").ap()
    outT = nc.dram_tensor("outT", [128, NCH, DC, 512], F32, kind="ExternalOutput").ap()

    kscr = nc.dram_tensor("kscr", [2, NSUPQ, 128, DC, 512], BF16, kind="Internal", addr_space="Shared").ap()
    vscr = nc.dram_tensor("vscr", [2, cfg.NOWN, D], BF16, kind="Internal", addr_space="Shared").ap()
    cc_in = nc.dram_tensor("cc_in", [1, 128], F32, kind="Internal").ap()
    cc_out = nc.dram_tensor("cc_out", [1, 128], F32, kind="Internal").ap()
    qscr = nc.dram_tensor("qscr", [NSUPQ, 128, DC, 512], BF16, kind="Internal").ap()

    es = ExitStack()
    sb = lambda n, s, d: es.enter_context(nc.sbuf_tensor(n, s, d))
    ps = lambda n, s: es.enter_context(nc.psum_tensor(n, s, F32))
    sem = lambda n: es.enter_context(nc.semaphore(n))

    with es:
        union1 = sb("union1", [128, 2 * DC * 1024], BF16)
        wk_sb = union1[:, 0:DC * 1024].rearrange("p (c d) -> p c d", c=DC)
        wv_sb = union1[:, DC * 1024:2 * DC * 1024].rearrange("p (c d) -> p c d", c=DC)
        arena = union1[:, 0:SEG * QT].rearrange("p (t q) -> p t q", t=SEG)
        union2 = sb("union2", [128, DC * 1024], BF16)
        wq_sb = union2.rearrange("p (c d) -> p c d", c=DC)
        u2_sb = union2.rearrange("p (c d) -> p c d", c=FFC)
        mq_sb = sb("mq_sb", [128, DC * 1024], BF16)
        masks_sb = mq_sb[:, 0:8 * QT].rearrange("p (j q) -> p j q", j=8)
        q_sb = mq_sb[:, 8 * QT:16 * QT].rearrange("p (c q) -> p c q", c=DC)
        union3 = sb("union3", [128, 4096], BF16)
        pout = union3[:, 0:2048].rearrange("p (s q) -> p s q", s=4)
        vout = union3[:, 2048:4096].rearrange("p (s d) -> p s d", s=2)
        attnb = union3.rearrange("p (c q) -> p c q", c=DC)

        wo_sb = sb("wo_sb", [128, DC, 1024], BF16)
        u1_sb = sb("u1_sb", [128, DC, cfg.RK], BF16)
        v1_sb = sb("v1_sb", [128, 2, cfg.FF], BF16)
        v2_sb = sb("v2_sb", [128, 2, D], BF16)
        b1_sb = sb("b1_sb", [128, FFC], F32)
        b2_sb = sb("b2_sb", [128, DC], F32)
        ones_sb = sb("ones_sb", [128, 1], BF16)
        cc_in_sb = sb("cc_in_sb", [1, 128], F32)
        onesrow = sb("onesrow", [1, 128], F32)

        kbuf = sb("kbuf", [128, 2, DC, 512], BF16)
        vbuf = sb("vbuf", [128, 2, 4, 512], BF16)
        xb = sb("xb", [128, 2, DC, 512], BF16)
        bfs0 = xb[:, 0, :, :]
        bfs1 = xb[:, 1, :, :]
        xin = sb("xin", [128, DC, 512], F32)
        acc = sb("acc", [128, DC, 512], F32)
        x1 = sb("x1", [128, DC, 512], F32)
        P_sb = sb("P_sb", [128, 2, 512], BF16)
        H_sb = sb("H_sb", [128, 2, 512], BF16)
        Qb_sb = P_sb
        st32 = sb("st32", [128, 3 * QT], F32)
        st_m = st32[0:1, 0:QT]
        st_r = st32[0:1, QT:2 * QT]
        st_mr = st32[0:1, 2 * QT:3 * QT]
        st_e2 = st32[32:33, 0:QT]
        st_r0 = st32[32:33, 0:QT]
        st_t = st32[32:33, QT:2 * QT]
        st_v = st32[32:33, 2 * QT:3 * QT]

        p01 = ps("p01", [128, 2, 512])
        p2 = ps("p2", [128, 512])
        p34 = ps("p34", [128, 2, 512])
        p56 = ps("p56", [128, 2, 512])
        p7 = ps("p7", [128, 512])

        pav = [p34[:, 0, :], p34[:, 1, :], p56[:, 0, :], p56[:, 1, :]]
        pm = [p34[:, 0, :], p34[:, 1, :]]
        pQ = [p56[:, 0, :], p56[:, 1, :]]
        pln1 = p56[:, 0, :][0:1, :]
        pln2 = p56[:, 1, :][0:1, :]
        pA = p56[:, 0, :]
        pB = p56[:, 1, :]
        psums = p2[0:1, :]

        s_winit = sem("s_winit")
        s_winit2 = sem("s_winit2")
        s_binit = sem("s_binit")
        s_xld = [sem("s_xld0"), sem("s_xld1")]
        c_xld2 = sem("c_xld2")
        s_mld = sem("s_mld")
        s_bar = sem("s_bar")
        s_pwr = [sem("s_pwr0"), sem("s_pwr1"), sem("s_pwr2"), sem("s_pwr3")]
        s_vwr = [sem("s_vwr0"), sem("s_vwr1")]
        s_kld = [sem("s_kld0"), sem("s_kld1")]
        s_vld = [sem("s_vld0"), sem("s_vld1")]
        s_qld = sem("s_qld")
        s_xinld = sem("s_xinld")
        s_outwr = sem("s_outwr")

        c_kgrp = sem("c_kgrp")
        c_kevac = sem("c_kevac")
        c_vgrp = sem("c_vgrp")
        c_vevac = sem("c_vevac")
        c_qgrp = sem("c_qgrp")
        c_qevac1 = sem("c_qevac1")
        c_sc = sem("c_sc")
        c_exp = sem("c_exp")
        c_mask = sem("c_mask")
        c_sumdone = sem("c_sumdone")
        c_avu = sem("c_avu")
        c_avd = sem("c_avd")
        c_pavfree = sem("c_pavfree")
        c_recip = sem("c_recip")
        c_bcA = sem("c_bcA")
        c_norm = sem("c_norm")
        c_wog = sem("c_wog")
        c_z = sem("c_z")
        c_zb = sem("c_zb")
        c_zsq = sem("c_zsq")
        c_lnred = sem("c_lnred")
        c_stats1 = sem("c_stats1")
        c_rstd = sem("c_rstd")
        c_statAB = sem("c_statAB")
        c_bcast = sem("c_bcast")
        c_lnnorm = sem("c_lnnorm")
        c_x1b = sem("c_x1b")
        c_u1g = sem("c_u1g")
        c_pevac = sem("c_pevac")
        c_v1g = sem("c_v1g")
        c_gelu = sem("c_gelu")
        c_u2done = sem("c_u2done")
        c_qevac = sem("c_qevac")
        c_v2g = sem("c_v2g")
        c_z2 = sem("c_z2")
        c_abst = sem("c_abst")

        N_WINIT = 3 * 16
        N_WINIT2 = 5 * 16
        N_BINIT = 2 * 16
        NKG = NSUPQ * DC
        NVG = NSUPQ * 4
        NQG = NSUPQ * DC

        def banded_j(i, t):
            off = t - (cfg.NST[i] - 8)
            return off if off >= 0 else None


        def seg0_groups(i):
            L = cfg.segs[i][0][1]
            rem = L
            g1 = min(2, rem); rem -= g1
            g5b = min(4, rem); rem -= g5b
            g5a = min(min(4, max(0, L - 16)), rem); rem -= g5a
            g2 = min(4, rem); rem -= g2
            g3 = min(3, rem); rem -= g3
            g4 = rem
            out, p = [], 0
            for n in (g1, g2, g3, g4, g5a, g5b):
                out.append((p, n)); p += n
            assert p == L
            return out

        def build_points():
            pts = []
            pending_seg0 = set(range(NCH))
            for i in range(NCH):
                if i in pending_seg0:
                    s0, ln = cfg.segs[i][0]
                    pts.append(("Aseg", i, 0, 0, ln, True))
                    pending_seg0.discard(i)
                    if len(cfg.segs[i]) == 1:
                        pts.append(("recip", i))
                pts.append(("B", i, 0))
                for si in range(1, len(cfg.segs[i])):
                    s0, ln = cfg.segs[i][si]
                    pts.append(("Aseg", i, si, 0, ln, True))
                    if si == len(cfg.segs[i]) - 1:
                        pts.append(("recip", i))
                    pts.append(("B", i, si))
                pts.append(("norm", i))
                if cfg.PHASES < 3:
                    continue
                nx = i + 1
                G = seg0_groups(nx) if nx < NCH else None
                if G:
                    pending_seg0.discard(nx)
                def A(k):
                    if G and G[k][1]:
                        a, n = G[k]
                        last = (a + n == cfg.segs[nx][0][1])
                        pts.append(("Aseg", nx, 0, a, a + n, last))
                A(0)
                pts.append(("Wo", i))
                A(1)
                pts.append(("LNred", i, 0))
                pts.append(("stats", i, 0))
                A(2)
                pts.append(("bcast", i, 0))
                pts.append(("x1norm", i))
                A(3)
                pts.append(("U1", i))
                pts.append(("V1U2", i))
                pts.append(("V2", i))
                A(4)
                pts.append(("LNred", i, 1))
                pts.append(("stats", i, 1))
                A(5)
                if G and len(cfg.segs[nx]) == 1:
                    pts.append(("recip", nx))
                pts.append(("bcast", i, 1))
                pts.append(("outnorm", i))
            return pts

        POINTS = build_points()

        with nc.Block() as block:

            @block.gpsimd
            def _(g):
                for w_ap, w_dst in [(Wq, wq_sb), (Wk, wk_sb), (Wv, wv_sb)]:
                    g.dma_start(w_dst, w_ap).then_inc(s_winit, 16)
                for jj in range(NSUPQ):
                    slot = jj % 2
                    if jj >= 2:
                        g.wait_ge(c_qgrp, DC * (jj - 1))
                    g.dma_start(xb[:, slot, :, :], xown[:, jj]).then_inc(s_xld[slot], 16)
                    if jj == 1:
                        g.dma_start(masks_sb, masks_in).then_inc(s_mld, 32)
                g.memset(cc_in_sb[:], 1.0)
                g.dma_start(cc_in, cc_in_sb[:]).then_inc(s_mld, 16)
                g.wait_ge(s_mld, 48)

                def cc_level(m):
                    for sl in range(4):
                        g.wait_ge(s_pwr[sl], 16 * 4 * (m + 1))
                    for sl in range(2):
                        g.wait_ge(s_vwr[sl], 16 * 2 * (m + 1))
                    g.collective_compute(
                        "AllReduce", ALU.add,
                        replica_groups=[[0, 1], [2, 3], [4, 5], [6, 7]],
                        ins=[cc_in[:, 16 * m:16 * (m + 1)]],
                        outs=[cc_out[:, 16 * m:16 * (m + 1)]],
                    ).then_inc(s_bar, 1)

                if cfg.PHASES < 2:
                    for m in range(NSUPQ):
                        cc_level(m)
                    return
                cc_level(0)
                gq_max = [0, 0, 0, 0]
                for i in range(NCH):
                    if i == 1:
                        for w_ap, w_dst in [(Wo, wo_sb[:]), (U1, u1_sb[:]),
                                            (V1, v1_sb[:]), (V2, v2_sb[:])]:
                            g.dma_start(w_dst, w_ap).then_inc(s_winit2, 16)
                        g.wait_ge(c_qgrp, NQG)
                        g.dma_start(u2_sb[:], U2).then_inc(s_winit2, 16)
                    if i > 0:
                        g.wait_ge(c_sumdone, i)
                    need = 4 * (i + 1)
                    for sl in range(4):
                        if need > gq_max[sl]:
                            g.wait_ge(s_pwr[sl], 16 * need)
                            gq_max[sl] = need
                    g.dma_start(q_sb, qscr[i]).then_inc(s_qld, 16)
                    if i + 1 < NSUPQ:
                        cc_level(i + 1)
                    if cfg.PHASES < 3:
                        g.wait_ge(c_norm, i + 1)
                        g.dma_start(outT[:, i], attnb).then_inc(s_outwr, 16)
                        continue
                    if i > 0:
                        g.wait_ge(c_lnnorm, 2 * i)
                        g.dma_start(outT[:, i - 1], xin[:]).then_inc(s_outwr, 16)
                        g.wait_ge(s_outwr, 16 * i)
                    g.dma_start(xin[:], xown[:, i]).then_inc(s_xinld, 16)
                if cfg.PHASES >= 3:
                    g.wait_ge(c_lnnorm, 2 * NCH)
                    g.dma_start(outT[:, NCH - 1], xin[:]).then_inc(s_outwr, 16)
                g.wait_ge(s_outwr, 16 * NCH)

            @block.sync
            def _(sy):
                sy.dma_start(b1_sb[:], b1).then_inc(s_binit, 16)
                sy.dma_start(b2_sb[:], b2).then_inc(s_binit, 16)
                pid_s = sy.partition_id()
                for arm in nc.Switch(engines=[sy], index=[pid_s % 2], n=2):
                    for jj in range(NSUPQ):
                        for c2 in range(DC):
                            e = jj * 16 + c2
                            sy.wait_ge(c_kevac, jj * DC + c2 + 1)
                            sy.dma_start(kscr[arm, jj, :, c2, :], pout[:, e % 4, :]
                                         ).then_inc(s_pwr[e % 4], 16)
                        for c2 in range(DC):
                            e = jj * 16 + 8 + c2
                            sy.wait_ge(c_qevac1, jj * DC + c2 + 1)
                            sy.dma_start(qscr[jj, :, c2, :], pout[:, e % 4, :]
                                         ).then_inc(s_pwr[e % 4], 16)
                if cfg.PHASES < 2:
                    return
                bar_max = 0
                for i in range(NCH):
                    for si, (s0, ln) in enumerate(cfg.segs[i]):
                        for kt in range(s0 // 4, (s0 + ln) // 4):
                            gkt = cfg.KT_CUM[i] + kt
                            slot = gkt % 2
                            shard = 0 if (kt % 2 == 1) else 1
                            jsh = kt // 2
                            if jsh + 1 > bar_max:
                                sy.wait_ge(s_bar, jsh + 1)
                                bar_max = jsh + 1
                            if gkt >= 2:
                                sy.wait_ge(c_sc, cfg.KT_LASTSC[gkt - 2])
                            sy.dma_start(kbuf[:, slot, :, :], kscr[shard, jsh]
                                         ).then_inc(s_kld[slot], 16)
                        for h in range(2):
                            for u in range(ln // 4):
                                vu = cfg.VU[(i, si, h)] + u
                                slot = vu % 2
                                tV = s0 // 4 + u
                                shard = 0 if (tV % 2 == 1) else 1
                                r0 = 512 * (tV // 2)
                                if tV // 2 + 1 > bar_max:
                                    sy.wait_ge(s_bar, tV // 2 + 1)
                                    bar_max = tV // 2 + 1
                                if vu >= 2:
                                    sy.wait_ge(c_avu, vu - 1)
                                sy.dma_start(
                                    vbuf[:, slot, :, :],
                                    vscr[shard, r0:r0 + 512, 512 * h:512 * (h + 1)]
                                    .rearrange("(a p) d -> p a d", p=128)
                                ).then_inc(s_vld[slot], 16)

            @block.tensor
            def _(t):
                t.wait_ge(s_winit, N_WINIT)
                pq = [p2, p7]
                for jj in range(NSUPQ):
                    slot = jj % 2
                    t.wait_ge(s_xld[slot], 16 * (jj // 2 + 1))
                    for c2 in range(DC):
                        gk = jj * DC + c2
                        if gk >= 2:
                            t.wait_ge(c_kevac, gk - 1)
                        for c in range(DC):
                            mm = t.matmul(p01[:, gk % 2, :],
                                          wk_sb[:, c, 128 * c2:128 * (c2 + 1)],
                                          xb[:, slot, c, :], start=(c == 0),
                                          stop=(c == DC - 1), skip_group_check=True)
                        mm.then_inc(c_kgrp, 1)
                    for ts in range(4):
                        gv = jj * 4 + ts
                        if gv >= 2:
                            t.wait_ge(c_vevac, gv - 1)
                        pv = p34 if gv % 2 == 0 else p56
                        for c in range(DC):
                            lhs = xb[:, slot, c, 128 * ts:128 * (ts + 1)]
                            t.matmul(pv[:, 0, :], lhs, wv_sb[:, c, 0:512],
                                     start=(c == 0), stop=(c == DC - 1),
                                     skip_group_check=True)
                            mm = t.matmul(pv[:, 1, :], lhs, wv_sb[:, c, 512:1024],
                                          start=(c == 0), stop=(c == DC - 1),
                                          skip_group_check=True)
                        mm.then_inc(c_vgrp, 1)
                    for c2 in range(DC):
                        gq = jj * DC + c2
                        if gq >= 2:
                            t.wait_ge(c_qevac1, gq - 1)
                        for c in range(DC):
                            mm = t.matmul(pq[gq % 2][:],
                                          wq_sb[:, c, 128 * c2:128 * (c2 + 1)],
                                          xb[:, slot, c, :], start=(c == 0),
                                          stop=(c == DC - 1), skip_group_check=True)
                        mm.then_inc(c_qgrp, 1)
                if cfg.PHASES < 2:
                    return
                t.wait_ge(c_qevac1, NQG)
                t.wait_ge(c_vevac, NVG)
                t.wait_ge(s_mld, 32)

                summed = {}
                scored = {}

                def emit_sums(i, tt):
                    g2 = cfg.CUM[i] + tt
                    bj = banded_j(i, tt)
                    if bj is None:
                        t.wait_ge(c_exp, g2 + 1)
                    else:
                        t.wait_ge(c_mask, 8 * i + bj + 1)
                    if tt == 0 and i > 0:
                        t.wait_ge(c_recip, i)
                    mm = t.matmul(psums, ones_sb[:], arena[:, tt % SEG, :],
                                  start=(tt == 0), stop=(tt == cfg.NST[i] - 1),
                                  skip_group_check=True)
                    if tt == cfg.NST[i] - 1:
                        mm.then_inc(c_sumdone, 1)

                for pt in POINTS:
                    kind = pt[0]
                    if kind == "Aseg":
                        _, i, si, a, b2_, fin = pt
                        s0, ln = cfg.segs[i][si]
                        if si == 0 and a == 0:
                            t.wait_ge(s_qld, 16 * (i + 1))
                        for tl in range(a, b2_):
                            tt = s0 + tl
                            g2 = cfg.CUM[i] + tt
                            if g2 >= 2:
                                t.wait_ge(c_exp, g2 - 1)
                            kt = tt // 4
                            gkt = cfg.KT_CUM[i] + kt
                            if tt % 4 == 0:
                                t.wait_ge(s_kld[gkt % 2], 16 * (gkt // 2 + 1))
                            u = tt % 4
                            for c in range(DC):
                                mm = t.matmul(p01[:, g2 % 2, :],
                                              kbuf[:, gkt % 2, c, 128 * u:128 * (u + 1)],
                                              q_sb[:, c, :], start=(c == 0),
                                              stop=(c == DC - 1), skip_group_check=True)
                            mm.then_inc(c_sc, 1)
                            scored[i] = tt + 1
                            summed.setdefault(i, 0)
                            while summed[i] < scored[i] - 1:
                                emit_sums(i, summed[i])
                                summed[i] += 1
                        if fin:
                            while summed[i] < scored[i]:
                                emit_sums(i, summed[i])
                                summed[i] += 1
                    elif kind == "B":
                        _, i, si = pt
                        s0, ln = cfg.segs[i][si]
                        last_seg = (si == len(cfg.segs[i]) - 1)
                        for h in range(2):
                            shg = cfg.SH_CUM[i] + 2 * si + h
                            if si == 0 and h == 0:
                                if i > 0:
                                    t.wait_ge(c_z2, DC * i)
                                    t.wait_ge(c_abst, 2 * i)
                            if h == 0:
                                t.wait_ge(c_exp, cfg.CUM[i] + s0 + ln)
                                if last_seg:
                                    t.wait_ge(c_mask, 8 * (i + 1))
                            nu = ln // 4
                            for u in range(nu):
                                vu = cfg.VU[(i, si, h)] + u
                                t.wait_ge(s_vld[vu % 2], 16 * (vu // 2 + 1))
                                for jdv in range(4):
                                    if u == 0 and shg > 0:
                                        t.wait_ge(c_pavfree, 4 * shg + jdv - 3)
                                    for tq in range(4):
                                        tt = s0 + 4 * u + tq
                                        mm = t.matmul(pav[jdv],
                                                      vbuf[:, vu % 2, tq,
                                                           128 * jdv:128 * (jdv + 1)],
                                                      arena[:, tt % SEG, :],
                                                      start=(u == 0 and tq == 0),
                                                      stop=(u == nu - 1 and tq == 3),
                                                      skip_group_check=True)
                                    if u == nu - 1 and jdv < 3:
                                        mm.then_inc(c_avd, 1)
                                mm.then_inc(c_avu, 1)
                    elif kind == "recip":
                        i = pt[1]
                        t.wait_ge(c_recip, i + 1)
                        if i > 0:
                            t.wait_ge(c_norm, i)
                        t.matmul(p7[:], onesrow[:], st_m[:], start=True, stop=True,
                                 skip_group_check=True).then_inc(c_bcA, 1)
                    elif kind == "Wo":
                        i = pt[1]
                        for do in range(DC):
                            czt = DC * i + do
                            if do == 0:
                                if i == 0:
                                    t.wait_ge(s_winit2, 4 * 16)
                                t.wait_ge(c_norm, i + 1)
                                t.wait_ge(c_pavfree, 4 * cfg.SH_CUM[i + 1])
                            if do >= 2:
                                t.wait_ge(c_z, czt - 1)
                            for dv in range(DC):
                                mm = t.matmul(pm[do % 2], wo_sb[:, dv, 128 * do:128 * (do + 1)],
                                              attnb[:, dv, :], start=(dv == 0),
                                              stop=(dv == DC - 1), skip_group_check=True)
                            mm.then_inc(c_wog, 1)
                    elif kind == "LNred":
                        _, i, half = pt
                        jn = 2 * i + half
                        t.wait_ge(c_zb, jn + 1)
                        t.wait_ge(c_zsq, jn + 1)
                        if half == 0 and i > 0:
                            t.wait_ge(c_abst, jn)
                        for c in range(DC):
                            t.matmul(pln1, ones_sb[:], bfs0[:, c, :], start=(c == 0),
                                     stop=(c == DC - 1), skip_group_check=True)
                        for c in range(DC):
                            mm = t.matmul(pln2, ones_sb[:], bfs1[:, c, :], start=(c == 0),
                                          stop=(c == DC - 1), skip_group_check=True)
                        mm.then_inc(c_lnred, 1)
                    elif kind == "bcast":
                        _, i, half = pt
                        jn = 2 * i + half
                        t.wait_ge(c_statAB, jn + 1)
                        t.matmul(pA, onesrow[:], st_r[:], start=True, stop=True,
                                 skip_group_check=True)
                        t.matmul(pB, onesrow[:], st_mr[:], start=True, stop=True,
                                 skip_group_check=True).then_inc(c_bcast, 1)
                    elif kind == "U1":
                        i = pt[1]
                        t.wait_ge(c_x1b, i + 1)
                        for r2 in range(2):
                            for c in range(DC):
                                mm = t.matmul(pm[r2], u1_sb[:, c, 128 * r2:128 * (r2 + 1)],
                                              bfs0[:, c, :], start=(c == 0),
                                              stop=(c == DC - 1), skip_group_check=True)
                            mm.then_inc(c_u1g, 1)
                    elif kind == "V1U2":
                        i = pt[1]
                        GB = FFC * i
                        if i == 0:
                            t.wait_ge(s_winit2, N_WINIT2)
                        t.wait_ge(c_pevac, 2 * (i + 1))
                        t.wait_ge(c_abst, 2 * i + 1)
                        for f in range(FFC):
                            if f >= 2:
                                t.wait_ge(c_gelu, GB + f - 1)
                            for rc in range(2):
                                mm = t.matmul(pm[f % 2], v1_sb[:, rc, 128 * f:128 * (f + 1)],
                                              P_sb[:, rc, :], start=(rc == 0), stop=(rc == 1),
                                              skip_group_check=True)
                            mm.then_inc(c_v1g, 1)
                            if f >= 1:
                                fu = f - 1
                                t.wait_ge(c_gelu, GB + fu + 1)
                                for r2 in range(2):
                                    t.matmul(pQ[r2], u2_sb[:, fu, 128 * r2:128 * (r2 + 1)],
                                             H_sb[:, fu % 2, :], start=(fu == 0), stop=False,
                                             skip_group_check=True)
                        fu = FFC - 1
                        t.wait_ge(c_gelu, GB + fu + 1)
                        for r2 in range(2):
                            mm = t.matmul(pQ[r2], u2_sb[:, fu, 128 * r2:128 * (r2 + 1)],
                                          H_sb[:, fu % 2, :], start=False, stop=True,
                                          skip_group_check=True)
                        mm.then_inc(c_u2done, 1)
                    elif kind == "V2":
                        i = pt[1]
                        t.wait_ge(c_qevac, i + 1)
                        for c in range(DC):
                            gz = DC * i + c
                            if c >= 2:
                                t.wait_ge(c_z2, gz - 1)
                            for rc in range(2):
                                mm = t.matmul(pm[c % 2], v2_sb[:, rc, 128 * c:128 * (c + 1)],
                                              Qb_sb[:, rc, :], start=(rc == 0), stop=(rc == 1),
                                              skip_group_check=True)
                            mm.then_inc(c_v2g, 1)

            @block.scalar
            def _(s):
                s.wait_ge(s_winit, N_WINIT)
                s.wait_ge(s_binit, N_BINIT)
                pid_a = s.partition_id()
                for arm in nc.Switch(engines=[s], index=[pid_a % 2], n=2):
                    for jj in range(NSUPQ):
                        for ts in range(4):
                            gv = jj * 4 + ts
                            s.wait_ge(c_vevac, gv + 1)
                            s.dma_start(vscr[arm, 512 * jj + 128 * ts: 512 * jj + 128 * (ts + 1), :],
                                        vout[:, gv % 2, :]).then_inc(s_vwr[gv % 2], 16)
                if cfg.PHASES < 2:
                    return
                seg_started = set()
                for pt in POINTS:
                    kind = pt[0]
                    if kind == "Aseg":
                        _, i, si, a, b2_, fin = pt
                        s0, ln = cfg.segs[i][si]
                        for tl in range(a, b2_):
                            tt = s0 + tl
                            g2 = cfg.CUM[i] + tt
                            if (i, si) not in seg_started:
                                seg_started.add((i, si))
                                prev = cfg.SH_CUM[i] + 2 * si
                                if prev > 0:
                                    s.wait_ge(c_pavfree, 4 * prev)
                            s.wait_ge(c_sc, g2 + 1)
                            s.activation(arena[:, tt % SEG, :], p01[:, g2 % 2, :],
                                         AF.Exp, scale=1.0 / 32.0).then_inc(c_exp, 1)
                    elif kind == "stats" and cfg.PHASES >= 3:
                        _, i, half = pt
                        jn = 2 * i + half
                        s.wait_ge(c_stats1, jn + 1)
                        s.activation(st_t[:], st_v[:], AF.Ln)
                        s.activation(st_r[:], st_t[:], AF.Exp, scale=-0.5).then_inc(c_rstd, 1)
                    elif kind == "V1U2" and cfg.PHASES >= 3:
                        i = pt[1]
                        GB = FFC * i
                        for f in range(FFC):
                            s.wait_ge(c_v1g, GB + f + 1)
                            s.activation(H_sb[:, f % 2, :], pm[f % 2], AF.Gelu_apprx_tanh,
                                         bias=b1_sb[:, f:f + 1]).then_inc(c_gelu, 1)

            @block.vector
            def _(v):
                def emit_stats(jn):
                    v.wait_ge(c_lnred, jn + 1)
                    v.tensor_scalar_mul(st_m[:], pln1, 1.0 / 1024.0)
                    v.tensor_scalar_mul(st_e2[:], pln2, 1.0 / 1024.0)
                    v.tensor_mul(st_t[:], st_m[:], st_m[:])
                    v.scalar_tensor_tensor(st_v[:], st_e2[:], 1e-5, st_t[:],
                                           ALU.add, ALU.subtract).then_inc(c_stats1, 1)
                    v.wait_ge(c_rstd, jn + 1)
                    v.tensor_mul(st_mr[:], st_m[:], st_r[:]).then_inc(c_statAB, 1)

                v.wait_ge(s_winit, N_WINIT)
                v.wait_ge(s_binit, N_BINIT)
                v.memset(ones_sb[:], 1.0)
                v.memset(onesrow[:], 1.0)
                pqv = [p2, p7]
                for jj in range(NSUPQ):
                    for c2 in range(DC):
                        gk = jj * DC + c2
                        e = jj * 16 + c2
                        v.wait_ge(c_kgrp, gk + 1)
                        if e >= 4:
                            v.wait_ge(s_pwr[e % 4], 16 * (e // 4))
                        op = v.tensor_copy(pout[:, e % 4, :], p01[:, gk % 2, :])
                        op.then_inc(c_kevac, 1)
                    for ts in range(4):
                        gv = jj * 4 + ts
                        v.wait_ge(c_vgrp, gv + 1)
                        if gv >= 2:
                            v.wait_ge(s_vwr[gv % 2], 16 * (gv // 2))
                        pv = p34 if gv % 2 == 0 else p56
                        v.tensor_copy(vout[:, gv % 2, 0:512], pv[:, 0, :])
                        op = v.tensor_copy(vout[:, gv % 2, 512:1024], pv[:, 1, :])
                        op.then_inc(c_vevac, 1)
                    for c2 in range(DC):
                        gq = jj * DC + c2
                        e = jj * 16 + 8 + c2
                        v.wait_ge(c_qgrp, gq + 1)
                        if e >= 4:
                            v.wait_ge(s_pwr[e % 4], 16 * (e // 4))
                        op = v.tensor_copy(pout[:, e % 4, :], pqv[gq % 2][:])
                        op.then_inc(c_qevac1, 1)
                if cfg.PHASES < 2:
                    return
                for pt in POINTS:
                    kind = pt[0]
                    if kind == "Aseg":
                        _, i, si, a, b2_, fin = pt
                        s0, ln = cfg.segs[i][si]
                        if si != len(cfg.segs[i]) - 1:
                            continue
                        NST = cfg.NST[i]
                        for tl in range(a, b2_):
                            tt = s0 + tl
                            bj = banded_j(i, tt)
                            if bj is None:
                                continue
                            gmask = cfg.CUM[i] + tt
                            v.wait_ge(c_exp, gmask + 1)
                            v.tensor_mul(arena[:, tt % SEG, :], arena[:, tt % SEG, :],
                                         masks_sb[:, bj, :]).then_inc(c_mask, 1)
                    elif kind == "B":
                        _, i, si = pt
                        s0, ln = cfg.segs[i][si]
                        for h in range(2):
                            shg = cfg.SH_CUM[i] + 2 * si + h
                            for jdv in range(4):
                                if jdv < 3:
                                    v.wait_ge(c_avd, 3 * shg + jdv + 1)
                                else:
                                    v.wait_ge(c_avu, cfg.VU[(i, si, h)] + ln // 4)
                                cdv = 4 * h + jdv
                                if si == 0:
                                    op = v.tensor_copy(acc[:, cdv, :], pav[jdv])
                                else:
                                    op = v.tensor_add(acc[:, cdv, :], acc[:, cdv, :], pav[jdv])
                                op.then_inc(c_pavfree, 1)
                    elif kind == "recip":
                        i = pt[1]
                        v.wait_ge(c_sumdone, i + 1)
                        v.reciprocal(st_m[:], psums).then_inc(c_recip, 1)
                    elif kind == "norm":
                        i = pt[1]
                        v.wait_ge(c_bcA, i + 1)
                        for c in range(DC):
                            op = v.tensor_mul(attnb[:, c, :], acc[:, c, :], p7[:])
                        op.then_inc(c_norm, 1)
                    elif kind == "Wo":
                        i = pt[1]
                        v.wait_ge(s_xinld, 16 * (i + 1))
                        for do in range(DC):
                            czt = DC * i + do
                            v.wait_ge(c_wog, czt + 1)
                            v.tensor_add(acc[:, do, :], pm[do % 2], xin[:, do, :]).then_inc(c_z, 1)
                            cp = v.tensor_copy(bfs0[:, do, :], acc[:, do, :])
                            mu = v.tensor_mul(bfs1[:, do, :], acc[:, do, :], acc[:, do, :])
                        cp.then_inc(c_zb, 1)
                        mu.then_inc(c_zsq, 1)
                    elif kind == "stats":
                        _, i, half = pt
                        emit_stats(2 * i + half)
                    elif kind == "x1norm":
                        i = pt[1]
                        v.wait_ge(c_bcast, 2 * i + 1)
                        v.tensor_copy(xin[:, 0, :], pA)
                        v.tensor_copy(xin[:, 1, :], pB).then_inc(c_abst, 1)
                        for c in range(DC):
                            v.tensor_mul(x1[:, c, :], acc[:, c, :], xin[:, 0, :])
                            op = v.tensor_sub(x1[:, c, :], x1[:, c, :], xin[:, 1, :])
                            if c == DC - 1:
                                op.then_inc(c_lnnorm, 1)
                            op = v.tensor_copy(bfs0[:, c, :], x1[:, c, :])
                        op.then_inc(c_x1b, 1)
                    elif kind == "U1":
                        i = pt[1]
                        for r2 in range(2):
                            v.wait_ge(c_u1g, 2 * i + r2 + 1)
                            v.tensor_copy(P_sb[:, r2, :], pm[r2]).then_inc(c_pevac, 1)
                    elif kind == "V2":
                        i = pt[1]
                        v.wait_ge(c_u2done, i + 1)
                        for r2 in range(2):
                            op = v.tensor_copy(Qb_sb[:, r2, :], pQ[r2])
                        op.then_inc(c_qevac, 1)
                        for c in range(DC):
                            gz = DC * i + c
                            v.wait_ge(c_v2g, gz + 1)
                            v.scalar_tensor_tensor(acc[:, c, :], pm[c % 2], b2_sb[:, c:c + 1],
                                                   x1[:, c, :], ALU.add, ALU.add).then_inc(c_z2, 1)
                            cp = v.tensor_copy(bfs0[:, c, :], acc[:, c, :])
                            mu = v.tensor_mul(bfs1[:, c, :], acc[:, c, :], acc[:, c, :])
                        cp.then_inc(c_zb, 1)
                        mu.then_inc(c_zsq, 1)
                    elif kind == "outnorm":
                        i = pt[1]
                        v.wait_ge(c_bcast, 2 * i + 2)
                        v.tensor_copy(x1[:, 0, :], pA)
                        v.tensor_copy(x1[:, 1, :], pB).then_inc(c_abst, 1)
                        for c in range(DC):
                            v.tensor_mul(xin[:, c, :], acc[:, c, :], x1[:, 0, :])
                            op = v.tensor_sub(xin[:, c, :], xin[:, c, :], x1[:, 1, :])
                        op.then_inc(c_lnnorm, 1)

    return nc



def pmaj(w):
    w = np.asarray(w, np.float32)
    C = w.shape[0] // 128
    return np.ascontiguousarray(w.reshape(C, 128, w.shape[1]).transpose(1, 0, 2))


def tok_pmaj(xtok):
    T = xtok.shape[0]
    return np.ascontiguousarray(xtok.reshape(T // 512, 512, 8, 128).transpose(3, 0, 2, 1))


def stage_weights(Wq, Wk, Wv, Wo, U1, V1, b1, U2, V2, b2):
    return {
        "Wq": pmaj(Wq), "Wk": pmaj(Wk), "Wv": pmaj(Wv), "Wo": pmaj(Wo),
        "U1": pmaj(U1), "V1": pmaj(V1), "U2": pmaj(U2), "V2": pmaj(V2),
        "b1": np.ascontiguousarray(np.asarray(b1, np.float32).reshape(32, 128).T),
        "b2": np.ascontiguousarray(np.asarray(b2, np.float32).reshape(8, 128).T),
    }


def _make_masks(parity: int, QT: int):
    ii = np.arange(128)[:, None]
    qq = np.arange(QT)[None, :]
    m = np.empty((8, 128, QT), dtype=np.float32)
    for j in range(8):
        if parity == 0:
            m[j] = 1.0 if j < 4 else (ii <= qq - 128 * (j - 4)).astype(np.float32)
        else:
            m[j] = (ii <= qq - 128 * j).astype(np.float32) if j < 4 else 0.0
    return m.astype(BF16_NP)


def kernel(L0_attn_x, L0_attn_Wq, L0_attn_Wk, L0_attn_Wv, L0_attn_Wo,
           L0_ffn_U1, L0_ffn_V1, L0_ffn_b1, L0_ffn_U2, L0_ffn_V2, L0_ffn_b2):
    x = np.ascontiguousarray(np.asarray(L0_attn_x, dtype=np.float32))
    B, S, D = x.shape
    cfg = Cfg(S=S)
    nc = build_nc(cfg)

    weights = stage_weights(L0_attn_Wq, L0_attn_Wk, L0_attn_Wv, L0_attn_Wo,
                            L0_ffn_U1, L0_ffn_V1, L0_ffn_b1, L0_ffn_U2,
                            L0_ffn_V2, L0_ffn_b2)
    masks = [np.ascontiguousarray(_make_masks(0, cfg.QT).transpose(1, 0, 2)),
             np.ascontiguousarray(_make_masks(1, cfg.QT).transpose(1, 0, 2))]

    n_cores = 8
    in_maps = []
    own_tiles = []
    for c in range(n_cores):
        b = c // 2
        par = c & 1
        xb_seq = x[b]
        tiles = [2 * i + (1 - par) for i in range(cfg.NCH)]
        own_tiles.append((b, tiles))
        xo = np.concatenate([xb_seq[512 * t:512 * (t + 1), :] for t in tiles], axis=0)
        in_maps.append({"xown": tok_pmaj(xo), "masks": masks[par], **weights})
    res = run_bass_kernel_spmd(nc, in_maps, core_ids=list(range(n_cores)))
    out = np.empty((B, S, D), dtype=np.float32)
    for c in range(n_cores):
        b, tiles = own_tiles[c]
        o4 = np.asarray(res.results[c]["outT"])
        o = o4.transpose(1, 3, 2, 0).reshape(cfg.NCH, 512, D)
        for idx, t in enumerate(tiles):
            out[b, 512 * t:512 * (t + 1), :] = o[idx]
    return out


# version 18
# speedup vs baseline: 1.0390x; 1.0390x over previous
import numpy as np
from contextlib import ExitStack

import concourse.bass as bass
import concourse.mybir as mybir
from concourse.bass_utils import run_bass_kernel_spmd

F32 = mybir.dt.float32
BF16 = mybir.dt.bfloat16
AF = mybir.ActivationFunctionType
ALU = mybir.AluOpType

try:
    import ml_dtypes
    BF16_NP = ml_dtypes.bfloat16
except Exception:
    BF16_NP = None


class Cfg:
    def __init__(self, S=8192, seg_tiles=32, phases=3):
        self.PHASES = phases
        self.D = 1024
        self.DC = 8
        self.S = S
        self.NOWN = S // 2
        self.QT = 512
        self.NCH = self.NOWN // self.QT
        self.NST = [(2 * self.QT // 128) * (i + 1) for i in range(self.NCH)]
        self.CUM = [0]
        for n in self.NST:
            self.CUM.append(self.CUM[-1] + n)
        self.SEG = seg_tiles
        self.segs = []
        for i in range(self.NCH):
            lst, t = [], 0
            while t < self.NST[i]:
                ln = min(self.SEG, self.NST[i] - t)
                lst.append((t, ln))
                t += ln
            self.segs.append(lst)
        self.SH_CUM = [0]
        for i in range(self.NCH):
            self.SH_CUM.append(self.SH_CUM[-1] + 2 * len(self.segs[i]))
        self.VU = {}
        vu = 0
        for i in range(self.NCH):
            for si, (s0, ln) in enumerate(self.segs[i]):
                for h in range(2):
                    self.VU[(i, si, h)] = vu
                    vu += ln // 4
        self.VU_TOT = vu
        self.NKT = [n // 4 for n in self.NST]
        self.KT_CUM = [0]
        for n in self.NKT:
            self.KT_CUM.append(self.KT_CUM[-1] + n)
        self.KT_LASTSC = []
        for i in range(self.NCH):
            for kt in range(self.NKT[i]):
                self.KT_LASTSC.append(self.CUM[i] + 4 * kt + 4)
        self.NSUP = S // 512
        self.NSUPQ = self.NOWN // 512
        self.FF = 4096
        self.FFC = 32
        self.RK = 256


def build_nc(cfg: Cfg):
    nc = bass.Bass("TRN2", target_bir_lowering=False, dynamic_dma_scratch_size=8192,
                   detect_race_conditions=False)
    D, DC, S = cfg.D, cfg.DC, cfg.S
    NCH, QT, SEG = cfg.NCH, cfg.QT, cfg.SEG
    NSUP, NSUPQ, FFC = cfg.NSUP, cfg.NSUPQ, cfg.FFC

    xown = nc.dram_tensor("xown", [128, NCH, DC, 512], F32, kind="ExternalInput").ap()
    Wq = nc.dram_tensor("Wq", [128, DC, D], F32, kind="ExternalInput").ap()
    Wk = nc.dram_tensor("Wk", [128, DC, D], F32, kind="ExternalInput").ap()
    Wv = nc.dram_tensor("Wv", [128, DC, D], F32, kind="ExternalInput").ap()
    Wo = nc.dram_tensor("Wo", [128, DC, D], F32, kind="ExternalInput").ap()
    U1 = nc.dram_tensor("U1", [128, DC, cfg.RK], F32, kind="ExternalInput").ap()
    V1 = nc.dram_tensor("V1", [128, 2, cfg.FF], F32, kind="ExternalInput").ap()
    b1 = nc.dram_tensor("b1", [128, cfg.FFC], F32, kind="ExternalInput").ap()
    U2 = nc.dram_tensor("U2", [128, FFC, cfg.RK], F32, kind="ExternalInput").ap()
    V2 = nc.dram_tensor("V2", [128, 2, D], F32, kind="ExternalInput").ap()
    b2 = nc.dram_tensor("b2", [128, DC], F32, kind="ExternalInput").ap()
    masks_in = nc.dram_tensor("masks", [128, 8, QT], BF16, kind="ExternalInput").ap()
    outT = nc.dram_tensor("outT", [128, NCH, DC, 512], F32, kind="ExternalOutput").ap()

    kscr = nc.dram_tensor("kscr", [2, NSUPQ, 128, DC, 512], BF16, kind="Internal", addr_space="Shared").ap()
    vscr = nc.dram_tensor("vscr", [2, cfg.NOWN, D], BF16, kind="Internal", addr_space="Shared").ap()
    cc_in = nc.dram_tensor("cc_in", [1, 128], F32, kind="Internal").ap()
    cc_out = nc.dram_tensor("cc_out", [1, 128], F32, kind="Internal").ap()
    qscr = nc.dram_tensor("qscr", [NSUPQ, 128, DC, 512], BF16, kind="Internal").ap()

    es = ExitStack()
    sb = lambda n, s, d: es.enter_context(nc.sbuf_tensor(n, s, d))
    ps = lambda n, s: es.enter_context(nc.psum_tensor(n, s, F32))
    sem = lambda n: es.enter_context(nc.semaphore(n))

    with es:
        union1 = sb("union1", [128, 2 * DC * 1024], BF16)
        wk_sb = union1[:, 0:DC * 1024].rearrange("p (c d) -> p c d", c=DC)
        wv_sb = union1[:, DC * 1024:2 * DC * 1024].rearrange("p (c d) -> p c d", c=DC)
        arena = union1[:, 0:SEG * QT].rearrange("p (t q) -> p t q", t=SEG)
        union2 = sb("union2", [128, DC * 1024], BF16)
        wq_sb = union2.rearrange("p (c d) -> p c d", c=DC)
        u2_sb = union2.rearrange("p (c d) -> p c d", c=FFC)
        mq_sb = sb("mq_sb", [128, DC * 1024], BF16)
        masks_sb = mq_sb[:, 0:8 * QT].rearrange("p (j q) -> p j q", j=8)
        q_sb = mq_sb[:, 8 * QT:16 * QT].rearrange("p (c q) -> p c q", c=DC)
        union3 = sb("union3", [128, 4096], BF16)
        pout = union3[:, 0:2048].rearrange("p (s q) -> p s q", s=4)
        vout = union3[:, 2048:4096].rearrange("p (s d) -> p s d", s=2)
        attnb = union3.rearrange("p (c q) -> p c q", c=DC)

        wo_sb = sb("wo_sb", [128, DC, 1024], BF16)
        u1_sb = sb("u1_sb", [128, DC, cfg.RK], BF16)
        v1_sb = sb("v1_sb", [128, 2, cfg.FF], BF16)
        v2_sb = sb("v2_sb", [128, 2, D], BF16)
        b1_sb = sb("b1_sb", [128, FFC], F32)
        b2_sb = sb("b2_sb", [128, DC], F32)
        ones_sb = sb("ones_sb", [128, 1], BF16)
        cc_in_sb = sb("cc_in_sb", [1, 128], F32)
        onesrow = sb("onesrow", [1, 128], F32)

        kbuf = sb("kbuf", [128, 2, DC, 512], BF16)
        vbuf = sb("vbuf", [128, 2, 4, 512], BF16)
        xb = sb("xb", [128, 2, DC, 512], BF16)
        bfs0 = xb[:, 0, :, :]
        bfs1 = xb[:, 1, :, :]
        xin = sb("xin", [128, DC, 512], F32)
        acc = sb("acc", [128, DC, 512], F32)
        x1 = sb("x1", [128, DC, 512], F32)
        P_sb = sb("P_sb", [128, 2, 512], BF16)
        H_sb = sb("H_sb", [128, 2, 512], BF16)
        Qb_sb = P_sb
        st32 = sb("st32", [128, 3 * QT], F32)
        st_m = st32[0:1, 0:QT]
        st_r = st32[0:1, QT:2 * QT]
        st_mr = st32[0:1, 2 * QT:3 * QT]
        st_e2 = st32[32:33, 0:QT]
        st_r0 = st32[32:33, 0:QT]
        st_t = st32[32:33, QT:2 * QT]
        st_v = st32[32:33, 2 * QT:3 * QT]

        p01 = ps("p01", [128, 2, 512])
        p2 = ps("p2", [128, 512])
        p34 = ps("p34", [128, 2, 512])
        p56 = ps("p56", [128, 2, 512])
        p7 = ps("p7", [128, 512])

        pav = [p34[:, 0, :], p34[:, 1, :], p56[:, 0, :], p56[:, 1, :]]
        pm = [p34[:, 0, :], p34[:, 1, :]]
        pQ = [p56[:, 0, :], p56[:, 1, :]]
        pln1 = p56[:, 0, :][0:1, :]
        pln2 = p56[:, 1, :][0:1, :]
        pA = p56[:, 0, :]
        pB = p56[:, 1, :]
        psums = p2[0:1, :]

        s_winit = sem("s_winit")
        s_winit2 = sem("s_winit2")
        s_binit = sem("s_binit")
        s_xld = [sem("s_xld0"), sem("s_xld1")]
        c_xld2 = sem("c_xld2")
        s_mld = sem("s_mld")
        s_bar = sem("s_bar")
        s_pwr = [sem("s_pwr0"), sem("s_pwr1"), sem("s_pwr2"), sem("s_pwr3")]
        s_vwr = [sem("s_vwr0"), sem("s_vwr1")]
        s_kld = [sem("s_kld0"), sem("s_kld1")]
        s_vld = [sem("s_vld0"), sem("s_vld1")]
        s_qld = sem("s_qld")
        s_xinld = sem("s_xinld")
        s_outwr = sem("s_outwr")

        c_kgrp = sem("c_kgrp")
        c_kevac = sem("c_kevac")
        c_vgrp = sem("c_vgrp")
        c_vevac = sem("c_vevac")
        c_qgrp = sem("c_qgrp")
        c_qevac1 = sem("c_qevac1")
        c_sc = sem("c_sc")
        c_exp = sem("c_exp")
        c_mask = sem("c_mask")
        c_sumdone = sem("c_sumdone")
        c_avu = sem("c_avu")
        c_avd = sem("c_avd")
        c_pavfree = sem("c_pavfree")
        c_recip = sem("c_recip")
        c_bcA = sem("c_bcA")
        c_norm = sem("c_norm")
        c_wog = sem("c_wog")
        c_z = sem("c_z")
        c_zb = sem("c_zb")
        c_zsq = sem("c_zsq")
        c_lnred = sem("c_lnred")
        c_stats1 = sem("c_stats1")
        c_rstd = sem("c_rstd")
        c_statAB = sem("c_statAB")
        c_bcast = sem("c_bcast")
        c_lnnorm = sem("c_lnnorm")
        c_x1b = sem("c_x1b")
        c_u1g = sem("c_u1g")
        c_pevac = sem("c_pevac")
        c_v1g = sem("c_v1g")
        c_gelu = sem("c_gelu")
        c_u2done = sem("c_u2done")
        c_qevac = sem("c_qevac")
        c_v2g = sem("c_v2g")
        c_z2 = sem("c_z2")
        c_abst = sem("c_abst")

        N_WINIT = 3 * 16
        N_WINIT2 = 5 * 16
        N_BINIT = 2 * 16
        NKG = NSUPQ * DC
        NVG = NSUPQ * 4
        NQG = NSUPQ * DC

        def banded_j(i, t):
            off = t - (cfg.NST[i] - 8)
            return off if off >= 0 else None


        def seg0_groups(i):
            L = cfg.segs[i][0][1]
            rem = L
            g1 = min(2, rem); rem -= g1
            g5b = min(4, rem); rem -= g5b
            g5a = min(min(4, max(0, L - 16)), rem); rem -= g5a
            g2 = min(4, rem); rem -= g2
            g3 = min(3, rem); rem -= g3
            g4 = rem
            out, p = [], 0
            for n in (g1, g2, g3, g4, g5a, g5b):
                out.append((p, n)); p += n
            assert p == L
            return out

        def build_points():
            pts = []
            pending_seg0 = set(range(NCH))
            for i in range(NCH):
                if i in pending_seg0:
                    s0, ln = cfg.segs[i][0]
                    pts.append(("Aseg", i, 0, 0, ln, True))
                    pending_seg0.discard(i)
                    if len(cfg.segs[i]) == 1:
                        pts.append(("recip", i))
                pts.append(("B", i, 0))
                for si in range(1, len(cfg.segs[i])):
                    s0, ln = cfg.segs[i][si]
                    pts.append(("Aseg", i, si, 0, ln, True))
                    if si == len(cfg.segs[i]) - 1:
                        pts.append(("recip", i))
                    pts.append(("B", i, si))
                pts.append(("norm", i))
                if cfg.PHASES < 3:
                    continue
                nx = i + 1
                G = seg0_groups(nx) if nx < NCH else None
                if G:
                    pending_seg0.discard(nx)
                def A(k):
                    if G and G[k][1]:
                        a, n = G[k]
                        last = (a + n == cfg.segs[nx][0][1])
                        pts.append(("Aseg", nx, 0, a, a + n, last))
                A(0)
                pts.append(("Wo", i))
                A(1)
                pts.append(("LNred", i, 0))
                pts.append(("stats", i, 0))
                A(2)
                pts.append(("bcast", i, 0))
                pts.append(("x1norm", i))
                A(3)
                pts.append(("U1", i))
                pts.append(("V1U2", i))
                pts.append(("V2", i))
                A(4)
                pts.append(("LNred", i, 1))
                pts.append(("stats", i, 1))
                A(5)
                if G and len(cfg.segs[nx]) == 1:
                    pts.append(("recip", nx))
                pts.append(("bcast", i, 1))
                pts.append(("outnorm", i))
            return pts

        POINTS = build_points()

        with nc.Block() as block:

            @block.gpsimd
            def _(g):
                def cc_level(m):
                    for sl in range(4):
                        g.wait_ge(s_pwr[sl], 16 * 4 * (m + 1))
                    for sl in range(2):
                        g.wait_ge(s_vwr[sl], 16 * 2 * (m + 1))
                    g.collective_compute(
                        "AllReduce", ALU.add,
                        replica_groups=[[0, 1], [2, 3], [4, 5], [6, 7]],
                        ins=[cc_in[:, 16 * m:16 * (m + 1)]],
                        outs=[cc_out[:, 16 * m:16 * (m + 1)]],
                    ).then_inc(s_bar, 1)

                g.memset(cc_in_sb[:], 1.0)
                g.dma_start(wk_sb, Wk).then_inc(s_winit, 16)
                for jj in range(NSUPQ):
                    slot = jj % 2
                    if jj >= 2:
                        g.wait_ge(c_qgrp, DC * (jj - 1))
                    g.dma_start(xb[:, slot, :, :], xown[:, jj]).then_inc(s_xld[slot], 16)
                    if jj == 0:
                        g.dma_start(wv_sb, Wv).then_inc(s_winit, 16)
                        g.dma_start(wq_sb, Wq).then_inc(s_winit, 16)
                        g.dma_start(cc_in, cc_in_sb[:]).then_inc(s_mld, 16)
                    if jj == 1:
                        g.dma_start(masks_sb, masks_in).then_inc(s_mld, 32)
                        g.wait_ge(s_mld, 48)
                    if jj >= 1:
                        cc_level(jj - 1)
                if cfg.PHASES < 2:
                    cc_level(NSUPQ - 1)
                    return
                gq_max = [0, 0, 0, 0]

                def qload(i):
                    need = 4 * (i + 1)
                    for sl in range(4):
                        if need > gq_max[sl]:
                            g.wait_ge(s_pwr[sl], 16 * need)
                            gq_max[sl] = need
                    g.dma_start(q_sb, qscr[i]).then_inc(s_qld, 16)

                qload(0)
                if cfg.PHASES < 3:
                    g.wait_ge(c_norm, 1)
                    g.dma_start(outT[:, 0], attnb).then_inc(s_outwr, 16)
                    cc_level(NSUPQ - 1)
                    for i in range(1, NCH):
                        g.wait_ge(c_sumdone, i)
                        qload(i)
                        g.wait_ge(c_norm, i + 1)
                        g.dma_start(outT[:, i], attnb).then_inc(s_outwr, 16)
                    g.wait_ge(s_outwr, 16 * NCH)
                    return
                g.dma_start(xin[:], xown[:, 0]).then_inc(s_xinld, 16)
                for w_ap, w_dst in [(Wo, wo_sb[:]), (U1, u1_sb[:]),
                                    (V1, v1_sb[:]), (V2, v2_sb[:])]:
                    g.dma_start(w_dst, w_ap).then_inc(s_winit2, 16)
                g.wait_ge(c_qgrp, NQG)
                g.dma_start(u2_sb[:], U2).then_inc(s_winit2, 16)
                cc_level(NSUPQ - 1)
                for i in range(1, NCH):
                    g.wait_ge(c_sumdone, i)
                    qload(i)
                    g.wait_ge(c_lnnorm, 2 * i)
                    g.dma_start(outT[:, i - 1], xin[:]).then_inc(s_outwr, 16)
                    g.wait_ge(s_outwr, 16 * i)
                    g.dma_start(xin[:], xown[:, i]).then_inc(s_xinld, 16)
                g.wait_ge(c_lnnorm, 2 * NCH)
                g.dma_start(outT[:, NCH - 1], xin[:]).then_inc(s_outwr, 16)
                g.wait_ge(s_outwr, 16 * NCH)

            @block.sync
            def _(sy):
                sy.dma_start(b1_sb[:], b1).then_inc(s_binit, 16)
                sy.dma_start(b2_sb[:], b2).then_inc(s_binit, 16)
                pid_s = sy.partition_id()
                for arm in nc.Switch(engines=[sy], index=[pid_s % 2], n=2):
                    for jj in range(NSUPQ):
                        for c2 in range(DC):
                            e = jj * 16 + c2
                            sy.wait_ge(c_kevac, jj * DC + c2 + 1)
                            sy.dma_start(kscr[arm, jj, :, c2, :], pout[:, e % 4, :]
                                         ).then_inc(s_pwr[e % 4], 16)
                        for c2 in range(DC):
                            e = jj * 16 + 8 + c2
                            sy.wait_ge(c_qevac1, jj * DC + c2 + 1)
                            sy.dma_start(qscr[jj, :, c2, :], pout[:, e % 4, :]
                                         ).then_inc(s_pwr[e % 4], 16)
                if cfg.PHASES < 2:
                    return
                bar_max = 0
                for i in range(NCH):
                    for si, (s0, ln) in enumerate(cfg.segs[i]):
                        for kt in range(s0 // 4, (s0 + ln) // 4):
                            gkt = cfg.KT_CUM[i] + kt
                            if gkt < 2:
                                continue
                            slot = gkt % 2
                            shard = 0 if (kt % 2 == 1) else 1
                            jsh = kt // 2
                            if jsh + 1 > bar_max:
                                sy.wait_ge(s_bar, jsh + 1)
                                bar_max = jsh + 1
                            sy.wait_ge(c_sc, cfg.KT_LASTSC[gkt - 2])
                            sy.dma_start(kbuf[:, slot, :, :], kscr[shard, jsh]
                                         ).then_inc(s_kld[slot], 16)
                        for h in range(2):
                            for u in range(ln // 4):
                                vu = cfg.VU[(i, si, h)] + u
                                if vu < 2:
                                    continue
                                slot = vu % 2
                                tV = s0 // 4 + u
                                shard = 0 if (tV % 2 == 1) else 1
                                r0 = 512 * (tV // 2)
                                if tV // 2 + 1 > bar_max:
                                    sy.wait_ge(s_bar, tV // 2 + 1)
                                    bar_max = tV // 2 + 1
                                sy.wait_ge(c_avu, vu - 1)
                                sy.dma_start(
                                    vbuf[:, slot, :, :],
                                    vscr[shard, r0:r0 + 512, 512 * h:512 * (h + 1)]
                                    .rearrange("(a p) d -> p a d", p=128)
                                ).then_inc(s_vld[slot], 16)

            @block.tensor
            def _(t):
                pq = [p2, p7]
                for jj in range(NSUPQ):
                    slot = jj % 2
                    if jj == 0:
                        t.wait_ge(s_winit, 16)
                    t.wait_ge(s_xld[slot], 16 * (jj // 2 + 1))
                    for c2 in range(DC):
                        gk = jj * DC + c2
                        if gk >= 2:
                            t.wait_ge(c_kevac, gk - 1)
                        for c in range(DC):
                            mm = t.matmul(p01[:, gk % 2, :],
                                          wk_sb[:, c, 128 * c2:128 * (c2 + 1)],
                                          xb[:, slot, c, :], start=(c == 0),
                                          stop=(c == DC - 1), skip_group_check=True)
                        mm.then_inc(c_kgrp, 1)
                    for ts in range(4):
                        gv = jj * 4 + ts
                        if gv == 0:
                            t.wait_ge(s_winit, 32)
                        if gv >= 2:
                            t.wait_ge(c_vevac, gv - 1)
                        pv = p34 if gv % 2 == 0 else p56
                        for c in range(DC):
                            lhs = xb[:, slot, c, 128 * ts:128 * (ts + 1)]
                            t.matmul(pv[:, 0, :], lhs, wv_sb[:, c, 0:512],
                                     start=(c == 0), stop=(c == DC - 1),
                                     skip_group_check=True)
                            mm = t.matmul(pv[:, 1, :], lhs, wv_sb[:, c, 512:1024],
                                          start=(c == 0), stop=(c == DC - 1),
                                          skip_group_check=True)
                        mm.then_inc(c_vgrp, 1)
                    for c2 in range(DC):
                        gq = jj * DC + c2
                        if gq == 0:
                            t.wait_ge(s_winit, N_WINIT)
                        if gq >= 2:
                            t.wait_ge(c_qevac1, gq - 1)
                        for c in range(DC):
                            mm = t.matmul(pq[gq % 2][:],
                                          wq_sb[:, c, 128 * c2:128 * (c2 + 1)],
                                          xb[:, slot, c, :], start=(c == 0),
                                          stop=(c == DC - 1), skip_group_check=True)
                        mm.then_inc(c_qgrp, 1)
                if cfg.PHASES < 2:
                    return
                t.wait_ge(c_qevac1, NQG)
                t.wait_ge(c_vevac, NVG)
                t.wait_ge(s_mld, 32)

                summed = {}
                scored = {}

                def emit_sums(i, tt):
                    g2 = cfg.CUM[i] + tt
                    bj = banded_j(i, tt)
                    if bj is None:
                        t.wait_ge(c_exp, g2 + 1)
                    else:
                        t.wait_ge(c_mask, 8 * i + bj + 1)
                    if tt == 0 and i > 0:
                        t.wait_ge(c_recip, i)
                    mm = t.matmul(psums, ones_sb[:], arena[:, tt % SEG, :],
                                  start=(tt == 0), stop=(tt == cfg.NST[i] - 1),
                                  skip_group_check=True)
                    if tt == cfg.NST[i] - 1:
                        mm.then_inc(c_sumdone, 1)

                for pt in POINTS:
                    kind = pt[0]
                    if kind == "Aseg":
                        _, i, si, a, b2_, fin = pt
                        s0, ln = cfg.segs[i][si]
                        if si == 0 and a == 0:
                            t.wait_ge(s_qld, 16 * (i + 1))
                        for tl in range(a, b2_):
                            tt = s0 + tl
                            g2 = cfg.CUM[i] + tt
                            if g2 >= 2:
                                t.wait_ge(c_exp, g2 - 1)
                            kt = tt // 4
                            gkt = cfg.KT_CUM[i] + kt
                            if tt % 4 == 0:
                                t.wait_ge(s_kld[gkt % 2], 16 * (gkt // 2 + 1))
                            u = tt % 4
                            for c in range(DC):
                                mm = t.matmul(p01[:, g2 % 2, :],
                                              kbuf[:, gkt % 2, c, 128 * u:128 * (u + 1)],
                                              q_sb[:, c, :], start=(c == 0),
                                              stop=(c == DC - 1), skip_group_check=True)
                            mm.then_inc(c_sc, 1)
                            scored[i] = tt + 1
                            summed.setdefault(i, 0)
                            while summed[i] < scored[i] - 1:
                                emit_sums(i, summed[i])
                                summed[i] += 1
                        if fin:
                            while summed[i] < scored[i]:
                                emit_sums(i, summed[i])
                                summed[i] += 1
                    elif kind == "B":
                        _, i, si = pt
                        s0, ln = cfg.segs[i][si]
                        last_seg = (si == len(cfg.segs[i]) - 1)
                        for h in range(2):
                            shg = cfg.SH_CUM[i] + 2 * si + h
                            if si == 0 and h == 0:
                                if i > 0:
                                    t.wait_ge(c_z2, DC * i)
                                    t.wait_ge(c_abst, 2 * i)
                            if h == 0:
                                t.wait_ge(c_exp, cfg.CUM[i] + s0 + ln)
                                if last_seg:
                                    t.wait_ge(c_mask, 8 * (i + 1))
                            nu = ln // 4
                            for u in range(nu):
                                vu = cfg.VU[(i, si, h)] + u
                                t.wait_ge(s_vld[vu % 2], 16 * (vu // 2 + 1))
                                for jdv in range(4):
                                    if u == 0 and shg > 0:
                                        t.wait_ge(c_pavfree, 4 * shg + jdv - 3)
                                    for tq in range(4):
                                        tt = s0 + 4 * u + tq
                                        mm = t.matmul(pav[jdv],
                                                      vbuf[:, vu % 2, tq,
                                                           128 * jdv:128 * (jdv + 1)],
                                                      arena[:, tt % SEG, :],
                                                      start=(u == 0 and tq == 0),
                                                      stop=(u == nu - 1 and tq == 3),
                                                      skip_group_check=True)
                                    if u == nu - 1 and jdv < 3:
                                        mm.then_inc(c_avd, 1)
                                mm.then_inc(c_avu, 1)
                    elif kind == "recip":
                        i = pt[1]
                        t.wait_ge(c_recip, i + 1)
                        if i > 0:
                            t.wait_ge(c_norm, i)
                        t.matmul(p7[:], onesrow[:], st_m[:], start=True, stop=True,
                                 skip_group_check=True).then_inc(c_bcA, 1)
                    elif kind == "Wo":
                        i = pt[1]
                        for do in range(DC):
                            czt = DC * i + do
                            if do == 0:
                                if i == 0:
                                    t.wait_ge(s_winit2, 4 * 16)
                                t.wait_ge(c_norm, i + 1)
                                t.wait_ge(c_pavfree, 4 * cfg.SH_CUM[i + 1])
                            if do >= 2:
                                t.wait_ge(c_z, czt - 1)
                            for dv in range(DC):
                                mm = t.matmul(pm[do % 2], wo_sb[:, dv, 128 * do:128 * (do + 1)],
                                              attnb[:, dv, :], start=(dv == 0),
                                              stop=(dv == DC - 1), skip_group_check=True)
                            mm.then_inc(c_wog, 1)
                    elif kind == "LNred":
                        _, i, half = pt
                        jn = 2 * i + half
                        t.wait_ge(c_zb, jn + 1)
                        t.wait_ge(c_zsq, jn + 1)
                        if half == 0 and i > 0:
                            t.wait_ge(c_abst, jn)
                        for c in range(DC):
                            t.matmul(pln1, ones_sb[:], bfs0[:, c, :], start=(c == 0),
                                     stop=(c == DC - 1), skip_group_check=True)
                        for c in range(DC):
                            mm = t.matmul(pln2, ones_sb[:], bfs1[:, c, :], start=(c == 0),
                                          stop=(c == DC - 1), skip_group_check=True)
                        mm.then_inc(c_lnred, 1)
                    elif kind == "bcast":
                        _, i, half = pt
                        jn = 2 * i + half
                        t.wait_ge(c_statAB, jn + 1)
                        t.matmul(pA, onesrow[:], st_r[:], start=True, stop=True,
                                 skip_group_check=True)
                        t.matmul(pB, onesrow[:], st_mr[:], start=True, stop=True,
                                 skip_group_check=True).then_inc(c_bcast, 1)
                    elif kind == "U1":
                        i = pt[1]
                        t.wait_ge(c_x1b, i + 1)
                        for r2 in range(2):
                            for c in range(DC):
                                mm = t.matmul(pm[r2], u1_sb[:, c, 128 * r2:128 * (r2 + 1)],
                                              bfs0[:, c, :], start=(c == 0),
                                              stop=(c == DC - 1), skip_group_check=True)
                            mm.then_inc(c_u1g, 1)
                    elif kind == "V1U2":
                        i = pt[1]
                        GB = FFC * i
                        if i == 0:
                            t.wait_ge(s_winit2, N_WINIT2)
                        t.wait_ge(c_pevac, 2 * (i + 1))
                        t.wait_ge(c_abst, 2 * i + 1)
                        for f in range(FFC):
                            if f >= 2:
                                t.wait_ge(c_gelu, GB + f - 1)
                            for rc in range(2):
                                mm = t.matmul(pm[f % 2], v1_sb[:, rc, 128 * f:128 * (f + 1)],
                                              P_sb[:, rc, :], start=(rc == 0), stop=(rc == 1),
                                              skip_group_check=True)
                            mm.then_inc(c_v1g, 1)
                            if f >= 1:
                                fu = f - 1
                                t.wait_ge(c_gelu, GB + fu + 1)
                                for r2 in range(2):
                                    t.matmul(pQ[r2], u2_sb[:, fu, 128 * r2:128 * (r2 + 1)],
                                             H_sb[:, fu % 2, :], start=(fu == 0), stop=False,
                                             skip_group_check=True)
                        fu = FFC - 1
                        t.wait_ge(c_gelu, GB + fu + 1)
                        for r2 in range(2):
                            mm = t.matmul(pQ[r2], u2_sb[:, fu, 128 * r2:128 * (r2 + 1)],
                                          H_sb[:, fu % 2, :], start=False, stop=True,
                                          skip_group_check=True)
                        mm.then_inc(c_u2done, 1)
                    elif kind == "V2":
                        i = pt[1]
                        t.wait_ge(c_qevac, i + 1)
                        for c in range(DC):
                            gz = DC * i + c
                            if c >= 2:
                                t.wait_ge(c_z2, gz - 1)
                            for rc in range(2):
                                mm = t.matmul(pm[c % 2], v2_sb[:, rc, 128 * c:128 * (c + 1)],
                                              Qb_sb[:, rc, :], start=(rc == 0), stop=(rc == 1),
                                              skip_group_check=True)
                            mm.then_inc(c_v2g, 1)

            @block.scalar
            def _(s):
                s.wait_ge(s_winit, N_WINIT)
                s.wait_ge(s_binit, N_BINIT)
                pid_a = s.partition_id()
                for arm in nc.Switch(engines=[s], index=[pid_a % 2], n=2):
                    for jj in range(NSUPQ):
                        if cfg.PHASES >= 2 and jj >= 1:
                            for p in range(4):
                                if min(1 + p, NSUPQ - 1) != jj:
                                    continue
                                if p == 0:
                                    s.wait_ge(s_bar, 1)
                                if p < 2:
                                    s.dma_start(kbuf[:, p, :, :], kscr[1 - p, 0]
                                                ).then_inc(s_kld[p], 16)
                                else:
                                    s.dma_start(
                                        vbuf[:, p - 2, :, :],
                                        vscr[3 - p, 0:512, 0:512]
                                        .rearrange("(a p) d -> p a d", p=128)
                                    ).then_inc(s_vld[p - 2], 16)
                        for ts in range(4):
                            gv = jj * 4 + ts
                            s.wait_ge(c_vevac, gv + 1)
                            s.dma_start(vscr[arm, 512 * jj + 128 * ts: 512 * jj + 128 * (ts + 1), :],
                                        vout[:, gv % 2, :]).then_inc(s_vwr[gv % 2], 16)
                if cfg.PHASES < 2:
                    return
                seg_started = set()
                for pt in POINTS:
                    kind = pt[0]
                    if kind == "Aseg":
                        _, i, si, a, b2_, fin = pt
                        s0, ln = cfg.segs[i][si]
                        for tl in range(a, b2_):
                            tt = s0 + tl
                            g2 = cfg.CUM[i] + tt
                            if (i, si) not in seg_started:
                                seg_started.add((i, si))
                                prev = cfg.SH_CUM[i] + 2 * si
                                if prev > 0:
                                    s.wait_ge(c_pavfree, 4 * prev)
                            s.wait_ge(c_sc, g2 + 1)
                            s.activation(arena[:, tt % SEG, :], p01[:, g2 % 2, :],
                                         AF.Exp, scale=1.0 / 32.0).then_inc(c_exp, 1)
                    elif kind == "stats" and cfg.PHASES >= 3:
                        _, i, half = pt
                        jn = 2 * i + half
                        s.wait_ge(c_stats1, jn + 1)
                        s.activation(st_t[:], st_v[:], AF.Ln)
                        s.activation(st_r[:], st_t[:], AF.Exp, scale=-0.5).then_inc(c_rstd, 1)
                    elif kind == "V1U2" and cfg.PHASES >= 3:
                        i = pt[1]
                        GB = FFC * i
                        for f in range(FFC):
                            s.wait_ge(c_v1g, GB + f + 1)
                            s.activation(H_sb[:, f % 2, :], pm[f % 2], AF.Gelu_apprx_tanh,
                                         bias=b1_sb[:, f:f + 1]).then_inc(c_gelu, 1)

            @block.vector
            def _(v):
                def emit_stats(jn):
                    v.wait_ge(c_lnred, jn + 1)
                    v.tensor_scalar_mul(st_m[:], pln1, 1.0 / 1024.0)
                    v.tensor_scalar_mul(st_e2[:], pln2, 1.0 / 1024.0)
                    v.tensor_mul(st_t[:], st_m[:], st_m[:])
                    v.scalar_tensor_tensor(st_v[:], st_e2[:], 1e-5, st_t[:],
                                           ALU.add, ALU.subtract).then_inc(c_stats1, 1)
                    v.wait_ge(c_rstd, jn + 1)
                    v.tensor_mul(st_mr[:], st_m[:], st_r[:]).then_inc(c_statAB, 1)

                v.wait_ge(s_winit, 16)
                v.wait_ge(s_binit, N_BINIT)
                v.memset(ones_sb[:], 1.0)
                v.memset(onesrow[:], 1.0)
                pqv = [p2, p7]
                for jj in range(NSUPQ):
                    for c2 in range(DC):
                        gk = jj * DC + c2
                        e = jj * 16 + c2
                        v.wait_ge(c_kgrp, gk + 1)
                        if e >= 4:
                            v.wait_ge(s_pwr[e % 4], 16 * (e // 4))
                        op = v.tensor_copy(pout[:, e % 4, :], p01[:, gk % 2, :])
                        op.then_inc(c_kevac, 1)
                    for ts in range(4):
                        gv = jj * 4 + ts
                        v.wait_ge(c_vgrp, gv + 1)
                        if gv >= 2:
                            v.wait_ge(s_vwr[gv % 2], 16 * (gv // 2))
                        pv = p34 if gv % 2 == 0 else p56
                        v.tensor_copy(vout[:, gv % 2, 0:512], pv[:, 0, :])
                        op = v.tensor_copy(vout[:, gv % 2, 512:1024], pv[:, 1, :])
                        op.then_inc(c_vevac, 1)
                    for c2 in range(DC):
                        gq = jj * DC + c2
                        e = jj * 16 + 8 + c2
                        v.wait_ge(c_qgrp, gq + 1)
                        if e >= 4:
                            v.wait_ge(s_pwr[e % 4], 16 * (e // 4))
                        op = v.tensor_copy(pout[:, e % 4, :], pqv[gq % 2][:])
                        op.then_inc(c_qevac1, 1)
                if cfg.PHASES < 2:
                    return
                for pt in POINTS:
                    kind = pt[0]
                    if kind == "Aseg":
                        _, i, si, a, b2_, fin = pt
                        s0, ln = cfg.segs[i][si]
                        if si != len(cfg.segs[i]) - 1:
                            continue
                        NST = cfg.NST[i]
                        for tl in range(a, b2_):
                            tt = s0 + tl
                            bj = banded_j(i, tt)
                            if bj is None:
                                continue
                            gmask = cfg.CUM[i] + tt
                            v.wait_ge(c_exp, gmask + 1)
                            v.tensor_mul(arena[:, tt % SEG, :], arena[:, tt % SEG, :],
                                         masks_sb[:, bj, :]).then_inc(c_mask, 1)
                    elif kind == "B":
                        _, i, si = pt
                        s0, ln = cfg.segs[i][si]
                        for h in range(2):
                            shg = cfg.SH_CUM[i] + 2 * si + h
                            for jdv in range(4):
                                if jdv < 3:
                                    v.wait_ge(c_avd, 3 * shg + jdv + 1)
                                else:
                                    v.wait_ge(c_avu, cfg.VU[(i, si, h)] + ln // 4)
                                cdv = 4 * h + jdv
                                if si == 0:
                                    op = v.tensor_copy(acc[:, cdv, :], pav[jdv])
                                else:
                                    op = v.tensor_add(acc[:, cdv, :], acc[:, cdv, :], pav[jdv])
                                op.then_inc(c_pavfree, 1)
                    elif kind == "recip":
                        i = pt[1]
                        v.wait_ge(c_sumdone, i + 1)
                        v.reciprocal(st_m[:], psums).then_inc(c_recip, 1)
                    elif kind == "norm":
                        i = pt[1]
                        v.wait_ge(c_bcA, i + 1)
                        for c in range(DC):
                            op = v.tensor_mul(attnb[:, c, :], acc[:, c, :], p7[:])
                        op.then_inc(c_norm, 1)
                    elif kind == "Wo":
                        i = pt[1]
                        v.wait_ge(s_xinld, 16 * (i + 1))
                        for do in range(DC):
                            czt = DC * i + do
                            v.wait_ge(c_wog, czt + 1)
                            v.tensor_add(acc[:, do, :], pm[do % 2], xin[:, do, :]).then_inc(c_z, 1)
                            cp = v.tensor_copy(bfs0[:, do, :], acc[:, do, :])
                            mu = v.tensor_mul(bfs1[:, do, :], acc[:, do, :], acc[:, do, :])
                        cp.then_inc(c_zb, 1)
                        mu.then_inc(c_zsq, 1)
                    elif kind == "stats":
                        _, i, half = pt
                        emit_stats(2 * i + half)
                    elif kind == "x1norm":
                        i = pt[1]
                        v.wait_ge(c_bcast, 2 * i + 1)
                        v.tensor_copy(xin[:, 0, :], pA)
                        v.tensor_copy(xin[:, 1, :], pB).then_inc(c_abst, 1)
                        for c in range(DC):
                            v.tensor_mul(x1[:, c, :], acc[:, c, :], xin[:, 0, :])
                            op = v.tensor_sub(x1[:, c, :], x1[:, c, :], xin[:, 1, :])
                            if c == DC - 1:
                                op.then_inc(c_lnnorm, 1)
                            op = v.tensor_copy(bfs0[:, c, :], x1[:, c, :])
                        op.then_inc(c_x1b, 1)
                    elif kind == "U1":
                        i = pt[1]
                        for r2 in range(2):
                            v.wait_ge(c_u1g, 2 * i + r2 + 1)
                            v.tensor_copy(P_sb[:, r2, :], pm[r2]).then_inc(c_pevac, 1)
                    elif kind == "V2":
                        i = pt[1]
                        v.wait_ge(c_u2done, i + 1)
                        for r2 in range(2):
                            op = v.tensor_copy(Qb_sb[:, r2, :], pQ[r2])
                        op.then_inc(c_qevac, 1)
                        for c in range(DC):
                            gz = DC * i + c
                            v.wait_ge(c_v2g, gz + 1)
                            v.scalar_tensor_tensor(acc[:, c, :], pm[c % 2], b2_sb[:, c:c + 1],
                                                   x1[:, c, :], ALU.add, ALU.add).then_inc(c_z2, 1)
                            cp = v.tensor_copy(bfs0[:, c, :], acc[:, c, :])
                            mu = v.tensor_mul(bfs1[:, c, :], acc[:, c, :], acc[:, c, :])
                        cp.then_inc(c_zb, 1)
                        mu.then_inc(c_zsq, 1)
                    elif kind == "outnorm":
                        i = pt[1]
                        v.wait_ge(c_bcast, 2 * i + 2)
                        v.tensor_copy(x1[:, 0, :], pA)
                        v.tensor_copy(x1[:, 1, :], pB).then_inc(c_abst, 1)
                        for c in range(DC):
                            v.tensor_mul(xin[:, c, :], acc[:, c, :], x1[:, 0, :])
                            op = v.tensor_sub(xin[:, c, :], xin[:, c, :], x1[:, 1, :])
                        op.then_inc(c_lnnorm, 1)

    return nc



def pmaj(w):
    w = np.asarray(w, np.float32)
    C = w.shape[0] // 128
    return np.ascontiguousarray(w.reshape(C, 128, w.shape[1]).transpose(1, 0, 2))


def tok_pmaj(xtok):
    T = xtok.shape[0]
    return np.ascontiguousarray(xtok.reshape(T // 512, 512, 8, 128).transpose(3, 0, 2, 1))


def stage_weights(Wq, Wk, Wv, Wo, U1, V1, b1, U2, V2, b2):
    return {
        "Wq": pmaj(Wq), "Wk": pmaj(Wk), "Wv": pmaj(Wv), "Wo": pmaj(Wo),
        "U1": pmaj(U1), "V1": pmaj(V1), "U2": pmaj(U2), "V2": pmaj(V2),
        "b1": np.ascontiguousarray(np.asarray(b1, np.float32).reshape(32, 128).T),
        "b2": np.ascontiguousarray(np.asarray(b2, np.float32).reshape(8, 128).T),
    }


def _make_masks(parity: int, QT: int):
    ii = np.arange(128)[:, None]
    qq = np.arange(QT)[None, :]
    m = np.empty((8, 128, QT), dtype=np.float32)
    for j in range(8):
        if parity == 0:
            m[j] = 1.0 if j < 4 else (ii <= qq - 128 * (j - 4)).astype(np.float32)
        else:
            m[j] = (ii <= qq - 128 * j).astype(np.float32) if j < 4 else 0.0
    return m.astype(BF16_NP)


def kernel(L0_attn_x, L0_attn_Wq, L0_attn_Wk, L0_attn_Wv, L0_attn_Wo,
           L0_ffn_U1, L0_ffn_V1, L0_ffn_b1, L0_ffn_U2, L0_ffn_V2, L0_ffn_b2):
    x = np.ascontiguousarray(np.asarray(L0_attn_x, dtype=np.float32))
    B, S, D = x.shape
    cfg = Cfg(S=S)
    nc = build_nc(cfg)

    weights = stage_weights(L0_attn_Wq, L0_attn_Wk, L0_attn_Wv, L0_attn_Wo,
                            L0_ffn_U1, L0_ffn_V1, L0_ffn_b1, L0_ffn_U2,
                            L0_ffn_V2, L0_ffn_b2)
    masks = [np.ascontiguousarray(_make_masks(0, cfg.QT).transpose(1, 0, 2)),
             np.ascontiguousarray(_make_masks(1, cfg.QT).transpose(1, 0, 2))]

    n_cores = 8
    in_maps = []
    own_tiles = []
    for c in range(n_cores):
        b = c // 2
        par = c & 1
        xb_seq = x[b]
        tiles = [2 * i + (1 - par) for i in range(cfg.NCH)]
        own_tiles.append((b, tiles))
        xo = np.concatenate([xb_seq[512 * t:512 * (t + 1), :] for t in tiles], axis=0)
        in_maps.append({"xown": tok_pmaj(xo), "masks": masks[par], **weights})
    res = run_bass_kernel_spmd(nc, in_maps, core_ids=list(range(n_cores)))
    out = np.empty((B, S, D), dtype=np.float32)
    for c in range(n_cores):
        b, tiles = own_tiles[c]
        o4 = np.asarray(res.results[c]["outT"])
        o = o4.transpose(1, 3, 2, 0).reshape(cfg.NCH, 512, D)
        for idx, t in enumerate(tiles):
            out[b, 512 * t:512 * (t + 1), :] = o[idx]
    return out


# version 19
# speedup vs baseline: 1.0593x; 1.0195x over previous
import numpy as np
from contextlib import ExitStack

import concourse.bass as bass
import concourse.mybir as mybir
from concourse.bass_utils import run_bass_kernel_spmd

F32 = mybir.dt.float32
BF16 = mybir.dt.bfloat16
AF = mybir.ActivationFunctionType
ALU = mybir.AluOpType

try:
    import ml_dtypes
    BF16_NP = ml_dtypes.bfloat16
except Exception:
    BF16_NP = None


class Cfg:
    def __init__(self, S=8192, seg_tiles=32, phases=3):
        self.PHASES = phases
        self.D = 1024
        self.DC = 8
        self.S = S
        self.NOWN = S // 2
        self.QT = 512
        self.NCH = self.NOWN // self.QT
        self.NST = [(2 * self.QT // 128) * (i + 1) for i in range(self.NCH)]
        self.CUM = [0]
        for n in self.NST:
            self.CUM.append(self.CUM[-1] + n)
        self.SEG = seg_tiles
        self.segs = []
        for i in range(self.NCH):
            lst, t = [], 0
            while t < self.NST[i]:
                ln = min(self.SEG, self.NST[i] - t)
                lst.append((t, ln))
                t += ln
            self.segs.append(lst)
        self.SH_CUM = [0]
        for i in range(self.NCH):
            self.SH_CUM.append(self.SH_CUM[-1] + 2 * len(self.segs[i]))
        self.VU = {}
        vu = 0
        for i in range(self.NCH):
            for si, (s0, ln) in enumerate(self.segs[i]):
                for h in range(2):
                    self.VU[(i, si, h)] = vu
                    vu += ln // 4
        self.VU_TOT = vu
        self.NKT = [n // 4 for n in self.NST]
        self.KT_CUM = [0]
        for n in self.NKT:
            self.KT_CUM.append(self.KT_CUM[-1] + n)
        self.KT_LASTSC = []
        for i in range(self.NCH):
            for kt in range(self.NKT[i]):
                self.KT_LASTSC.append(self.CUM[i] + 4 * kt + 4)
        self.NSUP = S // 512
        self.NSUPQ = self.NOWN // 512
        self.FF = 4096
        self.FFC = 32
        self.RK = 256


def build_nc(cfg: Cfg):
    nc = bass.Bass("TRN2", target_bir_lowering=False, dynamic_dma_scratch_size=8192,
                   detect_race_conditions=False)
    D, DC, S = cfg.D, cfg.DC, cfg.S
    NCH, QT, SEG = cfg.NCH, cfg.QT, cfg.SEG
    NSUP, NSUPQ, FFC = cfg.NSUP, cfg.NSUPQ, cfg.FFC

    xown = nc.dram_tensor("xown", [128, NCH, DC, 512], F32, kind="ExternalInput").ap()
    Wq = nc.dram_tensor("Wq", [128, DC, D], F32, kind="ExternalInput").ap()
    Wk = nc.dram_tensor("Wk", [128, DC, D], F32, kind="ExternalInput").ap()
    Wv = nc.dram_tensor("Wv", [128, DC, D], F32, kind="ExternalInput").ap()
    Wo = nc.dram_tensor("Wo", [128, DC, D], F32, kind="ExternalInput").ap()
    U1 = nc.dram_tensor("U1", [128, DC, cfg.RK], F32, kind="ExternalInput").ap()
    V1 = nc.dram_tensor("V1", [128, 2, cfg.FF], F32, kind="ExternalInput").ap()
    b1 = nc.dram_tensor("b1", [128, cfg.FFC], F32, kind="ExternalInput").ap()
    U2 = nc.dram_tensor("U2", [128, FFC, cfg.RK], F32, kind="ExternalInput").ap()
    V2 = nc.dram_tensor("V2", [128, 2, D], F32, kind="ExternalInput").ap()
    b2 = nc.dram_tensor("b2", [128, DC], F32, kind="ExternalInput").ap()
    masks_in = nc.dram_tensor("masks", [128, 8, QT], BF16, kind="ExternalInput").ap()
    outT = nc.dram_tensor("outT", [128, NCH, DC, 512], F32, kind="ExternalOutput").ap()

    kscr = nc.dram_tensor("kscr", [2, NSUPQ, 128, DC, 512], BF16, kind="Internal", addr_space="Shared").ap()
    vscr = nc.dram_tensor("vscr", [2, cfg.NOWN, D], BF16, kind="Internal", addr_space="Shared").ap()
    cc_in = nc.dram_tensor("cc_in", [1, 128], F32, kind="Internal").ap()
    cc_out = nc.dram_tensor("cc_out", [1, 128], F32, kind="Internal").ap()
    qscr = nc.dram_tensor("qscr", [NSUPQ, 128, DC, 512], BF16, kind="Internal").ap()

    es = ExitStack()
    sb = lambda n, s, d: es.enter_context(nc.sbuf_tensor(n, s, d))
    ps = lambda n, s: es.enter_context(nc.psum_tensor(n, s, F32))
    sem = lambda n: es.enter_context(nc.semaphore(n))

    with es:
        union1 = sb("union1", [128, 2 * DC * 1024], BF16)
        wk_sb = union1[:, 0:DC * 1024].rearrange("p (c d) -> p c d", c=DC)
        wv_sb = union1[:, DC * 1024:2 * DC * 1024].rearrange("p (c d) -> p c d", c=DC)
        arena = union1[:, 0:SEG * QT].rearrange("p (t q) -> p t q", t=SEG)
        union2 = sb("union2", [128, DC * 1024], BF16)
        wq_sb = union2.rearrange("p (c d) -> p c d", c=DC)
        u2_sb = union2.rearrange("p (c d) -> p c d", c=FFC)
        mq_sb = sb("mq_sb", [128, DC * 1024], BF16)
        masks_sb = mq_sb[:, 0:8 * QT].rearrange("p (j q) -> p j q", j=8)
        q_sb = mq_sb[:, 8 * QT:16 * QT].rearrange("p (c q) -> p c q", c=DC)
        union3 = sb("union3", [128, 4096], BF16)
        pout = union3[:, 0:2048].rearrange("p (s q) -> p s q", s=4)
        vout = union3[:, 2048:4096].rearrange("p (s d) -> p s d", s=2)
        attnb = union3.rearrange("p (c q) -> p c q", c=DC)

        wo_sb = sb("wo_sb", [128, DC, 1024], BF16)
        u1_sb = sb("u1_sb", [128, DC, cfg.RK], BF16)
        v1_sb = sb("v1_sb", [128, 2, cfg.FF], BF16)
        v2_sb = sb("v2_sb", [128, 2, D], BF16)
        b1_sb = sb("b1_sb", [128, FFC], F32)
        b2_sb = sb("b2_sb", [128, DC], F32)
        ones_sb = sb("ones_sb", [128, 1], BF16)
        cc_in_sb = sb("cc_in_sb", [1, 128], F32)
        onesrow = sb("onesrow", [1, 128], F32)

        kbuf = sb("kbuf", [128, 2, DC, 512], BF16)
        vbuf = sb("vbuf", [128, 2, 4, 512], BF16)
        xb = sb("xb", [128, 2, DC, 512], BF16)
        bfs0 = xb[:, 0, :, :]
        bfs1 = xb[:, 1, :, :]
        xin = sb("xin", [128, DC, 512], F32)
        acc = sb("acc", [128, DC, 512], F32)
        x1 = sb("x1", [128, DC, 512], F32)
        P_sb = sb("P_sb", [128, 2, 512], BF16)
        H_sb = sb("H_sb", [128, 2, 512], BF16)
        Qb_sb = P_sb
        st32 = sb("st32", [128, 3 * QT], F32)
        st_m = st32[0:1, 0:QT]
        st_r = st32[0:1, QT:2 * QT]
        st_mr = st32[0:1, 2 * QT:3 * QT]
        st_e2 = st32[32:33, 0:QT]
        st_r0 = st32[32:33, 0:QT]
        st_t = st32[32:33, QT:2 * QT]
        st_v = st32[32:33, 2 * QT:3 * QT]

        p01 = ps("p01", [128, 2, 512])
        p2 = ps("p2", [128, 512])
        p34 = ps("p34", [128, 2, 512])
        p56 = ps("p56", [128, 2, 512])
        p7 = ps("p7", [128, 512])

        pav = [p34[:, 0, :], p34[:, 1, :], p56[:, 0, :], p56[:, 1, :]]
        pm = [p34[:, 0, :], p34[:, 1, :]]
        pQ = [p56[:, 0, :], p56[:, 1, :]]
        pln1 = p56[:, 0, :][0:1, :]
        pln2 = p56[:, 1, :][0:1, :]
        pA = p56[:, 0, :]
        pB = p56[:, 1, :]
        psums = p2[0:1, :]

        s_winit = sem("s_winit")
        s_winit2 = sem("s_winit2")
        s_binit = sem("s_binit")
        s_xld = [sem("s_xld0"), sem("s_xld1")]
        c_xld2 = sem("c_xld2")
        s_mld = sem("s_mld")
        s_bar = sem("s_bar")
        s_pwr = [sem("s_pwr0"), sem("s_pwr1"), sem("s_pwr2"), sem("s_pwr3")]
        s_vwr = [sem("s_vwr0"), sem("s_vwr1")]
        s_kld = [sem("s_kld0"), sem("s_kld1")]
        s_vld = [sem("s_vld0"), sem("s_vld1")]
        s_qld = sem("s_qld")
        s_xinld = sem("s_xinld")
        s_outwr = sem("s_outwr")

        c_kgrp = sem("c_kgrp")
        c_kevac = sem("c_kevac")
        c_vgrp = sem("c_vgrp")
        c_vevac = sem("c_vevac")
        c_qgrp = sem("c_qgrp")
        c_qevac1 = sem("c_qevac1")
        c_sc = sem("c_sc")
        c_exp = sem("c_exp")
        c_mask = sem("c_mask")
        c_sumdone = sem("c_sumdone")
        c_avu = sem("c_avu")
        c_avd = sem("c_avd")
        c_pavfree = sem("c_pavfree")
        c_recip = sem("c_recip")
        c_bcA = sem("c_bcA")
        c_norm = sem("c_norm")
        c_wog = sem("c_wog")
        c_z = sem("c_z")
        c_zb = sem("c_zb")
        c_zsq = sem("c_zsq")
        c_lnred = sem("c_lnred")
        c_stats1 = sem("c_stats1")
        c_rstd = sem("c_rstd")
        c_statAB = sem("c_statAB")
        c_bcast = sem("c_bcast")
        c_lnnorm = sem("c_lnnorm")
        c_x1b = sem("c_x1b")
        c_u1g = sem("c_u1g")
        c_pevac = sem("c_pevac")
        c_v1g = sem("c_v1g")
        c_gelu = sem("c_gelu")
        c_u2done = sem("c_u2done")
        c_qevac = sem("c_qevac")
        c_v2g = sem("c_v2g")
        c_z2 = sem("c_z2")
        c_abst = sem("c_abst")

        N_WINIT = 3 * 16
        N_WINIT2 = 5 * 16
        N_BINIT = 2 * 16
        NKG = NSUPQ * DC
        NVG = NSUPQ * 4
        NQG = NSUPQ * DC

        def banded_j(i, t):
            off = t - (cfg.NST[i] - 8)
            return off if off >= 0 else None


        def seg0_groups(i):
            L = cfg.segs[i][0][1]
            rem = L
            g1 = min(2, rem); rem -= g1
            g5b = min(4, rem); rem -= g5b
            g5a = min(min(4, max(0, L - 16)), rem); rem -= g5a
            g2 = min(4, rem); rem -= g2
            g3 = min(3, rem); rem -= g3
            g4 = rem
            out, p = [], 0
            for n in (g1, g2, g3, g4, g5a, g5b):
                out.append((p, n)); p += n
            assert p == L
            return out

        def build_points():
            pts = []
            pending_seg0 = set(range(NCH))
            for i in range(NCH):
                if i in pending_seg0:
                    s0, ln = cfg.segs[i][0]
                    pts.append(("Aseg", i, 0, 0, ln, True))
                    pending_seg0.discard(i)
                    if len(cfg.segs[i]) == 1:
                        pts.append(("recip", i))
                pts.append(("B", i, 0))
                for si in range(1, len(cfg.segs[i])):
                    s0, ln = cfg.segs[i][si]
                    pts.append(("Aseg", i, si, 0, ln, True))
                    if si == len(cfg.segs[i]) - 1:
                        pts.append(("recip", i))
                    pts.append(("B", i, si))
                pts.append(("norm", i))
                if cfg.PHASES < 3:
                    continue
                nx = i + 1
                G = seg0_groups(nx) if nx < NCH else None
                if G:
                    pending_seg0.discard(nx)
                def A(k):
                    if G and G[k][1]:
                        a, n = G[k]
                        last = (a + n == cfg.segs[nx][0][1])
                        pts.append(("Aseg", nx, 0, a, a + n, last))
                A(0)
                pts.append(("Wo", i))
                A(1)
                pts.append(("LNred", i, 0))
                pts.append(("stats", i, 0))
                A(2)
                pts.append(("bcast", i, 0))
                pts.append(("x1norm", i))
                A(3)
                pts.append(("U1", i))
                pts.append(("V1U2", i))
                pts.append(("V2", i))
                A(4)
                pts.append(("LNred", i, 1))
                pts.append(("stats", i, 1))
                A(5)
                if G and len(cfg.segs[nx]) == 1:
                    pts.append(("recip", nx))
                pts.append(("bcast", i, 1))
                pts.append(("outnorm", i))
            return pts

        POINTS = build_points()

        with nc.Block() as block:

            @block.gpsimd
            def _(g):
                def cc_level(m):
                    for sl in range(4):
                        g.wait_ge(s_pwr[sl], 16 * 4 * (m + 1))
                    for sl in range(2):
                        g.wait_ge(s_vwr[sl], 16 * 2 * (m + 1))
                    g.collective_compute(
                        "AllReduce", ALU.add,
                        replica_groups=[[0, 1], [2, 3], [4, 5], [6, 7]],
                        ins=[cc_in[:, 16 * m:16 * (m + 1)]],
                        outs=[cc_out[:, 16 * m:16 * (m + 1)]],
                    ).then_inc(s_bar, 1)

                g.memset(cc_in_sb[:], 1.0)
                g.dma_start(wk_sb, Wk).then_inc(s_winit, 16)
                for jj in range(NSUPQ):
                    slot = jj % 2
                    if jj >= 2:
                        g.wait_ge(c_qgrp, DC * (jj - 1))
                    g.dma_start(xb[:, slot, :, :], xown[:, jj]).then_inc(s_xld[slot], 16)
                    if jj == 0:
                        g.dma_start(wv_sb, Wv).then_inc(s_winit, 16)
                        g.dma_start(wq_sb, Wq).then_inc(s_winit, 16)
                        g.dma_start(cc_in, cc_in_sb[:]).then_inc(s_mld, 16)
                    if jj == 1:
                        g.dma_start(masks_sb, masks_in).then_inc(s_mld, 32)
                        g.wait_ge(s_mld, 48)
                    if jj >= 1:
                        cc_level(jj - 1)
                if cfg.PHASES < 2:
                    cc_level(NSUPQ - 1)
                    return
                gq_max = [0, 0, 0, 0]

                def qload(i):
                    need = 4 * (i + 1)
                    for sl in range(4):
                        if need > gq_max[sl]:
                            g.wait_ge(s_pwr[sl], 16 * need)
                            gq_max[sl] = need
                    g.dma_start(q_sb, qscr[i]).then_inc(s_qld, 16)

                qload(0)
                if cfg.PHASES < 3:
                    g.wait_ge(c_norm, 1)
                    g.dma_start(outT[:, 0], attnb).then_inc(s_outwr, 16)
                    cc_level(NSUPQ - 1)
                    for i in range(1, NCH):
                        g.wait_ge(c_sumdone, i)
                        qload(i)
                        g.wait_ge(c_norm, i + 1)
                        g.dma_start(outT[:, i], attnb).then_inc(s_outwr, 16)
                    g.wait_ge(s_outwr, 16 * NCH)
                    return
                g.dma_start(xin[:], xown[:, 0]).then_inc(s_xinld, 16)
                for w_ap, w_dst in [(Wo, wo_sb[:]), (U1, u1_sb[:]),
                                    (V1, v1_sb[:]), (V2, v2_sb[:])]:
                    g.dma_start(w_dst, w_ap).then_inc(s_winit2, 16)
                g.wait_ge(c_qgrp, NQG)
                g.dma_start(u2_sb[:], U2).then_inc(s_winit2, 16)
                cc_level(NSUPQ - 1)
                for i in range(1, NCH):
                    g.wait_ge(c_sumdone, i)
                    qload(i)
                    g.wait_ge(c_lnnorm, 2 * i)
                    g.dma_start(outT[:, i - 1], xin[:]).then_inc(s_outwr, 16)
                    g.wait_ge(s_outwr, 16 * i)
                    g.dma_start(xin[:], xown[:, i]).then_inc(s_xinld, 16)
                g.wait_ge(c_lnnorm, 2 * NCH)
                g.dma_start(outT[:, NCH - 1], xin[:]).then_inc(s_outwr, 16)
                g.wait_ge(s_outwr, 16 * NCH)

            @block.sync
            def _(sy):
                sy.dma_start(b1_sb[:], b1).then_inc(s_binit, 16)
                sy.dma_start(b2_sb[:], b2).then_inc(s_binit, 16)
                pid_s = sy.partition_id()
                for arm in nc.Switch(engines=[sy], index=[pid_s % 2], n=2):
                    for jj in range(NSUPQ):
                        for c2 in range(DC):
                            e = jj * 16 + c2
                            sy.wait_ge(c_kevac, jj * DC + c2 + 1)
                            sy.dma_start(kscr[arm, jj, :, c2, :], pout[:, e % 4, :]
                                         ).then_inc(s_pwr[e % 4], 16)
                        for c2 in range(DC):
                            e = jj * 16 + 8 + c2
                            sy.wait_ge(c_qevac1, jj * DC + c2 + 1)
                            sy.dma_start(qscr[jj, :, c2, :], pout[:, e % 4, :]
                                         ).then_inc(s_pwr[e % 4], 16)
                if cfg.PHASES < 2:
                    return
                bar_max = 0
                for i in range(NCH):
                    for si, (s0, ln) in enumerate(cfg.segs[i]):
                        for kt in range(s0 // 4, (s0 + ln) // 4):
                            gkt = cfg.KT_CUM[i] + kt
                            if gkt < 2:
                                continue
                            slot = gkt % 2
                            shard = 0 if (kt % 2 == 1) else 1
                            jsh = kt // 2
                            if jsh + 1 > bar_max:
                                sy.wait_ge(s_bar, jsh + 1)
                                bar_max = jsh + 1
                            sy.wait_ge(c_sc, cfg.KT_LASTSC[gkt - 2])
                            sy.dma_start(kbuf[:, slot, :, :], kscr[shard, jsh]
                                         ).then_inc(s_kld[slot], 16)
                        for h in range(2):
                            for u in range(ln // 4):
                                vu = cfg.VU[(i, si, h)] + u
                                if vu < 2:
                                    continue
                                slot = vu % 2
                                tV = s0 // 4 + u
                                shard = 0 if (tV % 2 == 1) else 1
                                r0 = 512 * (tV // 2)
                                if tV // 2 + 1 > bar_max:
                                    sy.wait_ge(s_bar, tV // 2 + 1)
                                    bar_max = tV // 2 + 1
                                sy.wait_ge(c_avu, vu - 1)
                                sy.dma_start(
                                    vbuf[:, slot, :, :],
                                    vscr[shard, r0:r0 + 512, 512 * h:512 * (h + 1)]
                                    .rearrange("(a p) d -> p a d", p=128)
                                ).then_inc(s_vld[slot], 16)

            @block.tensor
            def _(t):
                pq = [p2, p7]
                for jj in range(NSUPQ):
                    slot = jj % 2
                    if jj == 0:
                        t.wait_ge(s_winit, 16)
                    t.wait_ge(s_xld[slot], 16 * (jj // 2 + 1))
                    for c2 in range(DC):
                        gk = jj * DC + c2
                        if gk >= 2:
                            t.wait_ge(c_kevac, gk - 1)
                        for c in range(DC):
                            mm = t.matmul(p01[:, gk % 2, :],
                                          wk_sb[:, c, 128 * c2:128 * (c2 + 1)],
                                          xb[:, slot, c, :], start=(c == 0),
                                          stop=(c == DC - 1), skip_group_check=True)
                        mm.then_inc(c_kgrp, 1)
                    for ts in range(4):
                        gv = jj * 4 + ts
                        if gv == 0:
                            t.wait_ge(s_winit, 32)
                        if gv >= 2:
                            t.wait_ge(c_vevac, gv - 1)
                        pv = p34 if gv % 2 == 0 else p56
                        for c in range(DC):
                            lhs = xb[:, slot, c, 128 * ts:128 * (ts + 1)]
                            t.matmul(pv[:, 0, :], lhs, wv_sb[:, c, 0:512],
                                     start=(c == 0), stop=(c == DC - 1),
                                     skip_group_check=True)
                            mm = t.matmul(pv[:, 1, :], lhs, wv_sb[:, c, 512:1024],
                                          start=(c == 0), stop=(c == DC - 1),
                                          skip_group_check=True)
                        mm.then_inc(c_vgrp, 1)
                    for c2 in range(DC):
                        gq = jj * DC + c2
                        if gq == 0:
                            t.wait_ge(s_winit, N_WINIT)
                        if gq >= 2:
                            t.wait_ge(c_qevac1, gq - 1)
                        for c in range(DC):
                            mm = t.matmul(pq[gq % 2][:],
                                          wq_sb[:, c, 128 * c2:128 * (c2 + 1)],
                                          xb[:, slot, c, :], start=(c == 0),
                                          stop=(c == DC - 1), skip_group_check=True)
                        mm.then_inc(c_qgrp, 1)
                if cfg.PHASES < 2:
                    return
                t.wait_ge(s_mld, 32)

                summed = {}
                scored = {}

                def emit_sums(i, tt):
                    g2 = cfg.CUM[i] + tt
                    if i == 0 and tt == 0:
                        t.wait_ge(c_qevac1, NQG)
                    bj = banded_j(i, tt)
                    if bj is None:
                        t.wait_ge(c_exp, g2 + 1)
                    else:
                        t.wait_ge(c_mask, 8 * i + bj + 1)
                    if tt == 0 and i > 0:
                        t.wait_ge(c_recip, i)
                    mm = t.matmul(psums, ones_sb[:], arena[:, tt % SEG, :],
                                  start=(tt == 0), stop=(tt == cfg.NST[i] - 1),
                                  skip_group_check=True)
                    if tt == cfg.NST[i] - 1:
                        mm.then_inc(c_sumdone, 1)

                for pt in POINTS:
                    kind = pt[0]
                    if kind == "Aseg":
                        _, i, si, a, b2_, fin = pt
                        s0, ln = cfg.segs[i][si]
                        if si == 0 and a == 0:
                            t.wait_ge(s_qld, 16 * (i + 1))
                        for tl in range(a, b2_):
                            tt = s0 + tl
                            g2 = cfg.CUM[i] + tt
                            if g2 >= 2:
                                t.wait_ge(c_exp, g2 - 1)
                            if g2 == 0:
                                t.wait_ge(c_kevac, NKG)
                            kt = tt // 4
                            gkt = cfg.KT_CUM[i] + kt
                            if tt % 4 == 0:
                                t.wait_ge(s_kld[gkt % 2], 16 * (gkt // 2 + 1))
                            u = tt % 4
                            for c in range(DC):
                                mm = t.matmul(p01[:, g2 % 2, :],
                                              kbuf[:, gkt % 2, c, 128 * u:128 * (u + 1)],
                                              q_sb[:, c, :], start=(c == 0),
                                              stop=(c == DC - 1), skip_group_check=True)
                            mm.then_inc(c_sc, 1)
                            scored[i] = tt + 1
                            summed.setdefault(i, 0)
                            while summed[i] < scored[i] - 1:
                                emit_sums(i, summed[i])
                                summed[i] += 1
                        if fin:
                            while summed[i] < scored[i]:
                                emit_sums(i, summed[i])
                                summed[i] += 1
                    elif kind == "B":
                        _, i, si = pt
                        s0, ln = cfg.segs[i][si]
                        last_seg = (si == len(cfg.segs[i]) - 1)
                        for h in range(2):
                            shg = cfg.SH_CUM[i] + 2 * si + h
                            if si == 0 and h == 0:
                                if i == 0:
                                    t.wait_ge(c_vevac, NVG)
                                if i > 0:
                                    t.wait_ge(c_z2, DC * i)
                                    t.wait_ge(c_abst, 2 * i)
                            if h == 0:
                                t.wait_ge(c_exp, cfg.CUM[i] + s0 + ln)
                                if last_seg:
                                    t.wait_ge(c_mask, 8 * (i + 1))
                            nu = ln // 4
                            for u in range(nu):
                                vu = cfg.VU[(i, si, h)] + u
                                t.wait_ge(s_vld[vu % 2], 16 * (vu // 2 + 1))
                                for jdv in range(4):
                                    if u == 0 and shg > 0:
                                        t.wait_ge(c_pavfree, 4 * shg + jdv - 3)
                                    for tq in range(4):
                                        tt = s0 + 4 * u + tq
                                        mm = t.matmul(pav[jdv],
                                                      vbuf[:, vu % 2, tq,
                                                           128 * jdv:128 * (jdv + 1)],
                                                      arena[:, tt % SEG, :],
                                                      start=(u == 0 and tq == 0),
                                                      stop=(u == nu - 1 and tq == 3),
                                                      skip_group_check=True)
                                    if u == nu - 1 and jdv < 3:
                                        mm.then_inc(c_avd, 1)
                                mm.then_inc(c_avu, 1)
                    elif kind == "recip":
                        i = pt[1]
                        t.wait_ge(c_recip, i + 1)
                        if i > 0:
                            t.wait_ge(c_norm, i)
                        t.matmul(p7[:], onesrow[:], st_m[:], start=True, stop=True,
                                 skip_group_check=True).then_inc(c_bcA, 1)
                    elif kind == "Wo":
                        i = pt[1]
                        for do in range(DC):
                            czt = DC * i + do
                            if do == 0:
                                if i == 0:
                                    t.wait_ge(s_winit2, 4 * 16)
                                t.wait_ge(c_norm, i + 1)
                                t.wait_ge(c_pavfree, 4 * cfg.SH_CUM[i + 1])
                            if do >= 2:
                                t.wait_ge(c_z, czt - 1)
                            for dv in range(DC):
                                mm = t.matmul(pm[do % 2], wo_sb[:, dv, 128 * do:128 * (do + 1)],
                                              attnb[:, dv, :], start=(dv == 0),
                                              stop=(dv == DC - 1), skip_group_check=True)
                            mm.then_inc(c_wog, 1)
                    elif kind == "LNred":
                        _, i, half = pt
                        jn = 2 * i + half
                        t.wait_ge(c_zb, jn + 1)
                        t.wait_ge(c_zsq, jn + 1)
                        if half == 0 and i > 0:
                            t.wait_ge(c_abst, jn)
                        for c in range(DC):
                            t.matmul(pln1, ones_sb[:], bfs0[:, c, :], start=(c == 0),
                                     stop=(c == DC - 1), skip_group_check=True)
                        for c in range(DC):
                            mm = t.matmul(pln2, ones_sb[:], bfs1[:, c, :], start=(c == 0),
                                          stop=(c == DC - 1), skip_group_check=True)
                        mm.then_inc(c_lnred, 1)
                    elif kind == "bcast":
                        _, i, half = pt
                        jn = 2 * i + half
                        t.wait_ge(c_statAB, jn + 1)
                        t.matmul(pA, onesrow[:], st_r[:], start=True, stop=True,
                                 skip_group_check=True)
                        t.matmul(pB, onesrow[:], st_mr[:], start=True, stop=True,
                                 skip_group_check=True).then_inc(c_bcast, 1)
                    elif kind == "U1":
                        i = pt[1]
                        t.wait_ge(c_x1b, i + 1)
                        for r2 in range(2):
                            for c in range(DC):
                                mm = t.matmul(pm[r2], u1_sb[:, c, 128 * r2:128 * (r2 + 1)],
                                              bfs0[:, c, :], start=(c == 0),
                                              stop=(c == DC - 1), skip_group_check=True)
                            mm.then_inc(c_u1g, 1)
                    elif kind == "V1U2":
                        i = pt[1]
                        GB = FFC * i
                        if i == 0:
                            t.wait_ge(s_winit2, N_WINIT2)
                        t.wait_ge(c_pevac, 2 * (i + 1))
                        t.wait_ge(c_abst, 2 * i + 1)
                        for f in range(FFC):
                            if f >= 2:
                                t.wait_ge(c_gelu, GB + f - 1)
                            for rc in range(2):
                                mm = t.matmul(pm[f % 2], v1_sb[:, rc, 128 * f:128 * (f + 1)],
                                              P_sb[:, rc, :], start=(rc == 0), stop=(rc == 1),
                                              skip_group_check=True)
                            mm.then_inc(c_v1g, 1)
                            if f >= 1:
                                fu = f - 1
                                t.wait_ge(c_gelu, GB + fu + 1)
                                for r2 in range(2):
                                    t.matmul(pQ[r2], u2_sb[:, fu, 128 * r2:128 * (r2 + 1)],
                                             H_sb[:, fu % 2, :], start=(fu == 0), stop=False,
                                             skip_group_check=True)
                        fu = FFC - 1
                        t.wait_ge(c_gelu, GB + fu + 1)
                        for r2 in range(2):
                            mm = t.matmul(pQ[r2], u2_sb[:, fu, 128 * r2:128 * (r2 + 1)],
                                          H_sb[:, fu % 2, :], start=False, stop=True,
                                          skip_group_check=True)
                        mm.then_inc(c_u2done, 1)
                    elif kind == "V2":
                        i = pt[1]
                        t.wait_ge(c_qevac, i + 1)
                        for c in range(DC):
                            gz = DC * i + c
                            if c >= 2:
                                t.wait_ge(c_z2, gz - 1)
                            for rc in range(2):
                                mm = t.matmul(pm[c % 2], v2_sb[:, rc, 128 * c:128 * (c + 1)],
                                              Qb_sb[:, rc, :], start=(rc == 0), stop=(rc == 1),
                                              skip_group_check=True)
                            mm.then_inc(c_v2g, 1)

            @block.scalar
            def _(s):
                s.wait_ge(s_winit, N_WINIT)
                s.wait_ge(s_binit, N_BINIT)
                pid_a = s.partition_id()
                for arm in nc.Switch(engines=[s], index=[pid_a % 2], n=2):
                    for jj in range(NSUPQ):
                        if cfg.PHASES >= 2 and jj >= 1:
                            for p in range(4):
                                if min(1 + p, NSUPQ - 1) != jj:
                                    continue
                                if p == 0:
                                    s.wait_ge(s_bar, 1)
                                if p < 2:
                                    s.dma_start(kbuf[:, p, :, :], kscr[1 - p, 0]
                                                ).then_inc(s_kld[p], 16)
                                else:
                                    s.dma_start(
                                        vbuf[:, p - 2, :, :],
                                        vscr[3 - p, 0:512, 0:512]
                                        .rearrange("(a p) d -> p a d", p=128)
                                    ).then_inc(s_vld[p - 2], 16)
                        for ts in range(4):
                            gv = jj * 4 + ts
                            s.wait_ge(c_vevac, gv + 1)
                            s.dma_start(vscr[arm, 512 * jj + 128 * ts: 512 * jj + 128 * (ts + 1), :],
                                        vout[:, gv % 2, :]).then_inc(s_vwr[gv % 2], 16)
                if cfg.PHASES < 2:
                    return
                seg_started = set()
                for pt in POINTS:
                    kind = pt[0]
                    if kind == "Aseg":
                        _, i, si, a, b2_, fin = pt
                        s0, ln = cfg.segs[i][si]
                        for tl in range(a, b2_):
                            tt = s0 + tl
                            g2 = cfg.CUM[i] + tt
                            if (i, si) not in seg_started:
                                seg_started.add((i, si))
                                prev = cfg.SH_CUM[i] + 2 * si
                                if prev > 0:
                                    s.wait_ge(c_pavfree, 4 * prev)
                            s.wait_ge(c_sc, g2 + 1)
                            s.activation(arena[:, tt % SEG, :], p01[:, g2 % 2, :],
                                         AF.Exp, scale=1.0 / 32.0).then_inc(c_exp, 1)
                    elif kind == "stats" and cfg.PHASES >= 3:
                        _, i, half = pt
                        jn = 2 * i + half
                        s.wait_ge(c_stats1, jn + 1)
                        s.activation(st_t[:], st_v[:], AF.Ln)
                        s.activation(st_r[:], st_t[:], AF.Exp, scale=-0.5).then_inc(c_rstd, 1)
                    elif kind == "V1U2" and cfg.PHASES >= 3:
                        i = pt[1]
                        GB = FFC * i
                        for f in range(FFC):
                            s.wait_ge(c_v1g, GB + f + 1)
                            s.activation(H_sb[:, f % 2, :], pm[f % 2], AF.Gelu_apprx_tanh,
                                         bias=b1_sb[:, f:f + 1]).then_inc(c_gelu, 1)

            @block.vector
            def _(v):
                def emit_stats(jn):
                    v.wait_ge(c_lnred, jn + 1)
                    v.tensor_scalar_mul(st_m[:], pln1, 1.0 / 1024.0)
                    v.tensor_scalar_mul(st_e2[:], pln2, 1.0 / 1024.0)
                    v.tensor_mul(st_t[:], st_m[:], st_m[:])
                    v.scalar_tensor_tensor(st_v[:], st_e2[:], 1e-5, st_t[:],
                                           ALU.add, ALU.subtract).then_inc(c_stats1, 1)
                    v.wait_ge(c_rstd, jn + 1)
                    v.tensor_mul(st_mr[:], st_m[:], st_r[:]).then_inc(c_statAB, 1)

                v.wait_ge(s_winit, 16)
                v.wait_ge(s_binit, N_BINIT)
                v.memset(ones_sb[:], 1.0)
                v.memset(onesrow[:], 1.0)
                pqv = [p2, p7]
                for jj in range(NSUPQ):
                    for c2 in range(DC):
                        gk = jj * DC + c2
                        e = jj * 16 + c2
                        v.wait_ge(c_kgrp, gk + 1)
                        if e >= 4:
                            v.wait_ge(s_pwr[e % 4], 16 * (e // 4))
                        op = v.tensor_copy(pout[:, e % 4, :], p01[:, gk % 2, :])
                        op.then_inc(c_kevac, 1)
                    for ts in range(4):
                        gv = jj * 4 + ts
                        v.wait_ge(c_vgrp, gv + 1)
                        if gv >= 2:
                            v.wait_ge(s_vwr[gv % 2], 16 * (gv // 2))
                        pv = p34 if gv % 2 == 0 else p56
                        v.tensor_copy(vout[:, gv % 2, 0:512], pv[:, 0, :])
                        op = v.tensor_copy(vout[:, gv % 2, 512:1024], pv[:, 1, :])
                        op.then_inc(c_vevac, 1)
                    for c2 in range(DC):
                        gq = jj * DC + c2
                        e = jj * 16 + 8 + c2
                        v.wait_ge(c_qgrp, gq + 1)
                        if e >= 4:
                            v.wait_ge(s_pwr[e % 4], 16 * (e // 4))
                        op = v.tensor_copy(pout[:, e % 4, :], pqv[gq % 2][:])
                        op.then_inc(c_qevac1, 1)
                if cfg.PHASES < 2:
                    return
                for pt in POINTS:
                    kind = pt[0]
                    if kind == "Aseg":
                        _, i, si, a, b2_, fin = pt
                        s0, ln = cfg.segs[i][si]
                        if si != len(cfg.segs[i]) - 1:
                            continue
                        NST = cfg.NST[i]
                        for tl in range(a, b2_):
                            tt = s0 + tl
                            bj = banded_j(i, tt)
                            if bj is None:
                                continue
                            gmask = cfg.CUM[i] + tt
                            v.wait_ge(c_exp, gmask + 1)
                            v.tensor_mul(arena[:, tt % SEG, :], arena[:, tt % SEG, :],
                                         masks_sb[:, bj, :]).then_inc(c_mask, 1)
                    elif kind == "B":
                        _, i, si = pt
                        s0, ln = cfg.segs[i][si]
                        for h in range(2):
                            shg = cfg.SH_CUM[i] + 2 * si + h
                            for jdv in range(4):
                                if jdv < 3:
                                    v.wait_ge(c_avd, 3 * shg + jdv + 1)
                                else:
                                    v.wait_ge(c_avu, cfg.VU[(i, si, h)] + ln // 4)
                                cdv = 4 * h + jdv
                                if si == 0:
                                    op = v.tensor_copy(acc[:, cdv, :], pav[jdv])
                                else:
                                    op = v.tensor_add(acc[:, cdv, :], acc[:, cdv, :], pav[jdv])
                                op.then_inc(c_pavfree, 1)
                    elif kind == "recip":
                        i = pt[1]
                        v.wait_ge(c_sumdone, i + 1)
                        v.reciprocal(st_m[:], psums).then_inc(c_recip, 1)
                    elif kind == "norm":
                        i = pt[1]
                        v.wait_ge(c_bcA, i + 1)
                        for c in range(DC):
                            op = v.tensor_mul(attnb[:, c, :], acc[:, c, :], p7[:])
                        op.then_inc(c_norm, 1)
                    elif kind == "Wo":
                        i = pt[1]
                        v.wait_ge(s_xinld, 16 * (i + 1))
                        for do in range(DC):
                            czt = DC * i + do
                            v.wait_ge(c_wog, czt + 1)
                            v.tensor_add(acc[:, do, :], pm[do % 2], xin[:, do, :]).then_inc(c_z, 1)
                            cp = v.tensor_copy(bfs0[:, do, :], acc[:, do, :])
                            mu = v.tensor_mul(bfs1[:, do, :], acc[:, do, :], acc[:, do, :])
                        cp.then_inc(c_zb, 1)
                        mu.then_inc(c_zsq, 1)
                    elif kind == "stats":
                        _, i, half = pt
                        emit_stats(2 * i + half)
                    elif kind == "x1norm":
                        i = pt[1]
                        v.wait_ge(c_bcast, 2 * i + 1)
                        v.tensor_copy(xin[:, 0, :], pA)
                        v.tensor_copy(xin[:, 1, :], pB).then_inc(c_abst, 1)
                        for c in range(DC):
                            v.tensor_mul(x1[:, c, :], acc[:, c, :], xin[:, 0, :])
                            op = v.tensor_sub(x1[:, c, :], x1[:, c, :], xin[:, 1, :])
                            if c == DC - 1:
                                op.then_inc(c_lnnorm, 1)
                            op = v.tensor_copy(bfs0[:, c, :], x1[:, c, :])
                        op.then_inc(c_x1b, 1)
                    elif kind == "U1":
                        i = pt[1]
                        for r2 in range(2):
                            v.wait_ge(c_u1g, 2 * i + r2 + 1)
                            v.tensor_copy(P_sb[:, r2, :], pm[r2]).then_inc(c_pevac, 1)
                    elif kind == "V2":
                        i = pt[1]
                        v.wait_ge(c_u2done, i + 1)
                        for r2 in range(2):
                            op = v.tensor_copy(Qb_sb[:, r2, :], pQ[r2])
                        op.then_inc(c_qevac, 1)
                        for c in range(DC):
                            gz = DC * i + c
                            v.wait_ge(c_v2g, gz + 1)
                            v.scalar_tensor_tensor(acc[:, c, :], pm[c % 2], b2_sb[:, c:c + 1],
                                                   x1[:, c, :], ALU.add, ALU.add).then_inc(c_z2, 1)
                            cp = v.tensor_copy(bfs0[:, c, :], acc[:, c, :])
                            mu = v.tensor_mul(bfs1[:, c, :], acc[:, c, :], acc[:, c, :])
                        cp.then_inc(c_zb, 1)
                        mu.then_inc(c_zsq, 1)
                    elif kind == "outnorm":
                        i = pt[1]
                        v.wait_ge(c_bcast, 2 * i + 2)
                        v.tensor_copy(x1[:, 0, :], pA)
                        v.tensor_copy(x1[:, 1, :], pB).then_inc(c_abst, 1)
                        for c in range(DC):
                            v.tensor_mul(xin[:, c, :], acc[:, c, :], x1[:, 0, :])
                            op = v.tensor_sub(xin[:, c, :], xin[:, c, :], x1[:, 1, :])
                        op.then_inc(c_lnnorm, 1)

    return nc



def pmaj(w):
    w = np.asarray(w, np.float32)
    C = w.shape[0] // 128
    return np.ascontiguousarray(w.reshape(C, 128, w.shape[1]).transpose(1, 0, 2))


def tok_pmaj(xtok):
    T = xtok.shape[0]
    return np.ascontiguousarray(xtok.reshape(T // 512, 512, 8, 128).transpose(3, 0, 2, 1))


def stage_weights(Wq, Wk, Wv, Wo, U1, V1, b1, U2, V2, b2):
    return {
        "Wq": pmaj(Wq), "Wk": pmaj(Wk), "Wv": pmaj(Wv), "Wo": pmaj(Wo),
        "U1": pmaj(U1), "V1": pmaj(V1), "U2": pmaj(U2), "V2": pmaj(V2),
        "b1": np.ascontiguousarray(np.asarray(b1, np.float32).reshape(32, 128).T),
        "b2": np.ascontiguousarray(np.asarray(b2, np.float32).reshape(8, 128).T),
    }


def _make_masks(parity: int, QT: int):
    ii = np.arange(128)[:, None]
    qq = np.arange(QT)[None, :]
    m = np.empty((8, 128, QT), dtype=np.float32)
    for j in range(8):
        if parity == 0:
            m[j] = 1.0 if j < 4 else (ii <= qq - 128 * (j - 4)).astype(np.float32)
        else:
            m[j] = (ii <= qq - 128 * j).astype(np.float32) if j < 4 else 0.0
    return m.astype(BF16_NP)


def kernel(L0_attn_x, L0_attn_Wq, L0_attn_Wk, L0_attn_Wv, L0_attn_Wo,
           L0_ffn_U1, L0_ffn_V1, L0_ffn_b1, L0_ffn_U2, L0_ffn_V2, L0_ffn_b2):
    x = np.ascontiguousarray(np.asarray(L0_attn_x, dtype=np.float32))
    B, S, D = x.shape
    cfg = Cfg(S=S)
    nc = build_nc(cfg)

    weights = stage_weights(L0_attn_Wq, L0_attn_Wk, L0_attn_Wv, L0_attn_Wo,
                            L0_ffn_U1, L0_ffn_V1, L0_ffn_b1, L0_ffn_U2,
                            L0_ffn_V2, L0_ffn_b2)
    masks = [np.ascontiguousarray(_make_masks(0, cfg.QT).transpose(1, 0, 2)),
             np.ascontiguousarray(_make_masks(1, cfg.QT).transpose(1, 0, 2))]

    n_cores = 8
    in_maps = []
    own_tiles = []
    for c in range(n_cores):
        b = c // 2
        par = c & 1
        xb_seq = x[b]
        tiles = [2 * i + (1 - par) for i in range(cfg.NCH)]
        own_tiles.append((b, tiles))
        xo = np.concatenate([xb_seq[512 * t:512 * (t + 1), :] for t in tiles], axis=0)
        in_maps.append({"xown": tok_pmaj(xo), "masks": masks[par], **weights})
    res = run_bass_kernel_spmd(nc, in_maps, core_ids=list(range(n_cores)))
    out = np.empty((B, S, D), dtype=np.float32)
    for c in range(n_cores):
        b, tiles = own_tiles[c]
        o4 = np.asarray(res.results[c]["outT"])
        o = o4.transpose(1, 3, 2, 0).reshape(cfg.NCH, 512, D)
        for idx, t in enumerate(tiles):
            out[b, 512 * t:512 * (t + 1), :] = o[idx]
    return out
